# version 09 (final)
# baseline (speedup 1.0000x reference)
import numpy as np
import ml_dtypes
import concourse.bass as bass
import concourse.mybir as mybir
from concourse.bass_utils import run_bass_kernel_spmd

F32 = mybir.dt.float32
BF16 = mybir.dt.bfloat16
I32 = mybir.dt.int32
AF = mybir.ActivationFunctionType
AX = mybir.AxisListType
ALU = mybir.AluOpType

B, S, E, H, D = 4, 4096, 1024, 16, 64
QT = 2048
SK = 4096
NHP = 4
NPASS = H // NHP
HID = 2048
NEXP = 8
EPS = 1e-5
MAGIC = 0x5F3759DF
NEWT = 3
TSF = 256
NTS = 2
QTR = 512
NQTR = QT // QTR
NEXPB = 3

ENGS = ("sync", "tensor", "scalar", "vector", "gpsimd")


class Sem:
    __slots__ = ("h", "n", "dma")

    def __init__(self, h, dma=False):
        self.h = h
        self.n = 0
        self.dma = dma

    def inc(self, k=1):
        self.n += k
        return (self, k)

    def at(self, v=None):
        if self.dma:
            return (self, self.n)
        return (self, self.n if v is None else v)


class Prog:
    def __init__(self):
        self.q = {e: [] for e in ENGS}

    def op(self, eng, fn=None, waits=(), incs=()):
        self.q[eng].append((fn, tuple(waits), tuple(incs)))

    def emit(self, block):
        for eng in ENGS:
            entries = self.q[eng]

            def make(entries=entries):
                def run(eh):
                    for fn, waits, incs in entries:
                        for (s, v) in waits:
                            if v > 0:
                                eh.wait_ge(s.h, v)
                        ins = fn(eh) if fn is not None else None
                        for (s, k) in incs:
                            assert ins is not None
                            ins.then_inc(s.h, k)
                return run
            getattr(block, eng)(make())


def build_program(upto="all", cut="all", qlim=99):
    CUT = {"stage": 1, "kmm": 2, "kev": 3, "v": 4, "q": 5, "q1": 5.1, "q2": 5.2,
           "q3": 5.3, "qo": 5.25, "all": 9}[cut]
    CUT = float(CUT)
    if cut == "qo":
        KON, VON = False, False
    else:
        KON, VON = CUT >= 2, CUT >= 4
    LV = {"su": 0, "pa": 1, "pb": 2, "att": 3, "pc": 4, "pg": 5,
          "f1": 6, "f2": 7, "f3": 8, "all": 9}[upto]
    NQ_RUN = {6: 1, 7: 1, 8: 1}.get(LV, NQTR)
    NE_RUN = {6: 1}.get(LV, NEXP)
    FIN_RUN = LV not in (6, 7)
    nc = bass.Bass("TRN2", target_bir_lowering=False, detect_race_conditions=False)

    d_xq = nc.dram_tensor("xq", [QT, E], F32, kind="ExternalInput").ap()
    d_xkv = nc.dram_tensor("xkv", [SK, E], F32, kind="ExternalInput").ap()
    d_Wq = nc.dram_tensor("Wq", [E, E], F32, kind="ExternalInput").ap()
    d_Wk = nc.dram_tensor("Wk", [E, E], F32, kind="ExternalInput").ap()
    d_Wv = nc.dram_tensor("Wv", [E, E], F32, kind="ExternalInput").ap()
    d_Wo = nc.dram_tensor("Wo", [E, E], F32, kind="ExternalInput").ap()
    d_bq = nc.dram_tensor("bq", [1, E], F32, kind="ExternalInput").ap()
    d_bk = nc.dram_tensor("bk", [1, E], F32, kind="ExternalInput").ap()
    d_bv = nc.dram_tensor("bv", [1, E], F32, kind="ExternalInput").ap()
    d_bo = nc.dram_tensor("bo", [1, E], F32, kind="ExternalInput").ap()
    d_J = nc.dram_tensor("J", [H, D, D], F32, kind="ExternalInput").ap()
    d_lam = nc.dram_tensor("lam", [1, 1], F32, kind="ExternalInput").ap()
    d_gw = nc.dram_tensor("gw", [E, NEXP], F32, kind="ExternalInput").ap()
    d_w1 = nc.dram_tensor("w1", [NEXP, E, HID], F32, kind="ExternalInput").ap()
    d_b1 = nc.dram_tensor("b1", [NEXP, HID], F32, kind="ExternalInput").ap()
    d_g1 = nc.dram_tensor("g1", [NEXP, HID], F32, kind="ExternalInput").ap()
    d_lb1 = nc.dram_tensor("lb1", [NEXP, HID], F32, kind="ExternalInput").ap()
    d_w2 = nc.dram_tensor("w2", [NEXP, HID, E], F32, kind="ExternalInput").ap()
    d_b2 = nc.dram_tensor("b2", [NEXP, E], F32, kind="ExternalInput").ap()
    d_lng = nc.dram_tensor("lng", [1, E], F32, kind="ExternalInput").ap()
    d_lnb = nc.dram_tensor("lnb", [1, E], F32, kind="ExternalInput").ap()
    d_out = nc.dram_tensor("out", [QT, E], F32, kind="ExternalOutput").ap()

    s_xkvbf = nc.dram_tensor("s_xkvbf", [SK, E], BF16, kind="Internal").ap()
    s_xqbf = nc.dram_tensor("s_xqbf", [QT, E], BF16, kind="Internal").ap()
    s_wbf = nc.dram_tensor("s_wbf", [4, E, E], BF16, kind="Internal").ap()
    s_w1bf = nc.dram_tensor("s_w1bf", [NEXP, E, HID], BF16, kind="Internal").ap()
    s_w2bf = nc.dram_tensor("s_w2bf", [NEXP, HID, E], BF16, kind="Internal").ap()
    s_xf32 = nc.dram_tensor("s_xf32", [QT, E], F32, kind="Internal").ap()
    s_xfbf = nc.dram_tensor("s_xfbf", [QT, E], BF16, kind="Internal").ap()
    s_xflo = nc.dram_tensor("s_xflo", [QT, E], BF16, kind="Internal").ap()

    eye1_f = nc.inline_tensor(np.eye(1, dtype=np.float32), name="eye1_f").ap()
    eye_f = nc.inline_tensor(np.eye(128, dtype=np.float32), name="eye_f").ap()
    ones1_f = nc.inline_tensor(np.ones((1, 128), np.float32), name="ones1_f").ap()
    ones128_bf = nc.inline_tensor(np.ones((128, 1), ml_dtypes.bfloat16),
                                  name="ones128_bf").ap()
    st2_np = np.zeros((64, 128), np.float32)
    st2_np[np.arange(64), np.arange(64)] = 1.0
    st2_np[np.arange(64), 64 + np.arange(64)] = 1.0
    st2_f = nc.inline_tensor(st2_np, name="st2_f").ap()
    sel_np = np.zeros((NEXP, NEXP * 128), np.float32)
    for e in range(NEXP):
        sel_np[e, e * 128:(e + 1) * 128] = 1.0
    sel8_f = nc.inline_tensor(sel_np, name="sel8_f").ap()

    sb_base = (nc.sbuf_base + 31) // 32 * 32
    usable = nc.sbuf_top - sb_base
    cursors = {}

    def region(name, start):
        cursors[name] = start

    def salloc(name, shape, dtype, reg):
        esz = {F32: 4, BF16: 2, I32: 4}[dtype]
        nb = int(np.prod(shape[1:])) * esz if len(shape) > 1 else esz
        nb = (nb + 31) // 32 * 32
        off = cursors[reg]
        assert off + nb <= usable, f"SBUF overflow: {name} reg={reg} end={off+nb} usable={usable}"
        cursors[reg] = off + nb
        return nc.alloc_sbuf_tensor_at(name, list(shape), dtype, offset=sb_base + off)

    region("p", 0)
    c_eye1 = salloc("c_eye1", [1, 1], F32, "p")
    c_eyef = salloc("c_eyef", [128, 128], F32, "p")
    c_ones1 = salloc("c_ones1", [1, 128], F32, "p")
    c_ones128 = salloc("c_ones128", [128, 1], BF16, "p")
    c_st2 = salloc("c_st2", [64, 128], F32, "p")
    c_sel8 = salloc("c_sel8", [NEXP, NEXP * 128], F32, "p")
    sb_J = salloc("sb_J", [64, H * 64], BF16, "p")
    sb_lam = salloc("sb_lam", [1, 1], F32, "p")
    sb_lamv = salloc("sb_lamv", [64, 1], F32, "p")
    sb_bqv = salloc("sb_bqv", [128, H], F32, "p")
    sb_bkv = salloc("sb_bkv", [128, H], F32, "p")
    sb_rowE = salloc("sb_rowE", [1, E], F32, "p")
    sb_b1v = salloc("sb_b1v", [128, NEXP * 16], F32, "p")
    sb_g1v = salloc("sb_g1v", [128, NEXP * 16], F32, "p")
    sb_lb1v = salloc("sb_lb1v", [128, NEXP * 16], F32, "p")
    sb_w1bar = salloc("sb_w1bar", [128, NEXP * 8], F32, "p")
    sb_w1barb = salloc("sb_w1barb", [128, NEXP * 8], BF16, "p")
    sb_b1bar = salloc("sb_b1bar", [1, NEXP], F32, "p")
    sb_b2 = salloc("sb_b2", [128, E], BF16, "p")
    sb_c8t = [salloc(f"sb_c8t{i}", [128, 128], BF16, "p") for i in range(16)]
    sb_gw = salloc("sb_gw", [128, 8 * NEXP], BF16, "p")
    sb_gwf = salloc("sb_gwf", [128, 8 * NEXP], F32, "p")
    sb_gwlo = salloc("sb_gwlo", [128, 8 * NEXP], BF16, "p")
    sb_c8 = salloc("sb_c8", [NEXP, QT], F32, "p")
    sb_colsc = salloc("sb_colsc", [128, 16], F32, "p")
    sb_svec = salloc("sb_svec", [128, 12], F32, "p")
    sb_xqt = salloc("sb_xqt", [128, 2 * E], F32, "p")
    PEND = cursors["p"]

    region("a", PEND)
    sb_wqt = salloc("sb_wqt", [128, 8 * 256], BF16, "a")
    sb_wkt = salloc("sb_wkt", [128, 8 * 256], BF16, "a")
    sb_wvt = salloc("sb_wvt", [128, 8 * 256], BF16, "a")
    sb_kp = salloc("sb_kp", [128, NHP * SK], BF16, "a")
    sb_vp = salloc("sb_vp", [128, 2 * 32 * 65 + 2 * 32 * 128], BF16, "a")
    sb_qp = salloc("sb_qp", [128, NHP * QT], BF16, "a")
    sb_xst = salloc("sb_xst", [128, 2 * 8 * 512], BF16, "a")
    sb_expt = salloc("sb_expt", [128, NEXPB * 1024], BF16, "a")
    sb_qn = salloc("sb_qn", [64, 512], BF16, "a")
    sb_rrow = salloc("sb_rrow", [1, 512], F32, "a")
    sb_rb = salloc("sb_rb", [128, 512], F32, "a")
    sb_att = salloc("sb_att", [128, 8 * QT], BF16, "a")
    sb_bob = salloc("sb_bob", [128, E], F32, "a")
    sb_xf = salloc("sb_xf", [128, E], F32, "a")
    sb_xfb = salloc("sb_xfb", [128, E], BF16, "a")
    sb_xlo = salloc("sb_xlo", [128, E], BF16, "a")
    sb_t0a = salloc("sb_t0a", [128, E], F32, "a")
    sb_scr = salloc("sb_scr", [128, E], F32, "a")
    AEND = cursors["a"]

    region("c", PEND + 3 * (8 * 256 * 2))
    sb_wot = salloc("sb_wot", [128, 8 * 1024], BF16, "c")

    region("f", PEND)
    sb_xfm = salloc("sb_xfm", [128, 8 * QT], BF16, "f")
    w1_off = cursors["f"]
    sb_w1 = salloc("sb_w1", [128, 8 * HID], BF16, "f")
    region("g2", w1_off)
    sb_xfmlo = salloc("sb_xfmlo", [128, 8 * QT], BF16, "g2")
    sb_w2 = salloc("sb_w2", [128, 16 * E], BF16, "f")
    y_off = cursors["f"]
    sb_y = salloc("sb_y", [128, 4 * E], F32, "f")
    region("s", y_off)
    sb_rows = salloc("sb_rows", [1, HID], F32, "s")
    sb_Jf = salloc("sb_Jf", [64, H * 64], F32, "s")
    sb_rows2 = salloc("sb_rows2", [1, HID], F32, "s")
    sb_h = salloc("sb_h", [128, 16 * TSF], F32, "f")
    sb_hsq = salloc("sb_hsq", [128, 3 * TSF], BF16, "f")
    sb_hp = salloc("sb_hp", [128, 16 * TSF], BF16, "f")
    sb_t1 = salloc("sb_t1", [128, 2 * TSF], F32, "f")
    sb_t2 = salloc("sb_t2", [128, 2 * TSF], F32, "f")
    sb_gg = salloc("sb_gg", [128, 2 * TSF], F32, "f")
    sb_mub = salloc("sb_mub", [128, TSF], F32, "f")
    sb_rstb = salloc("sb_rstb", [128, TSF], F32, "f")
    sb_cb = salloc("sb_cb", [128, TSF], F32, "f")
    sb_mrow = salloc("sb_mrow", [1, TSF], F32, "f")
    sb_srow = salloc("sb_srow", [1, TSF], F32, "f")
    sb_vrow = salloc("sb_vrow", [1, TSF], F32, "f")
    sb_yrow = salloc("sb_yrow", [1, TSF], F32, "f")
    sb_y2row = salloc("sb_y2row", [1, TSF], F32, "f")
    sb_gz = salloc("sb_gz", [128, 16], F32, "f")
    sb_t0f = salloc("sb_t0f", [128, E], F32, "f")
    sb_scrf = salloc("sb_scrf", [128, E], F32, "f")
    sb_outt = salloc("sb_outt", [128, E], F32, "f")
    sb_lngb = salloc("sb_lngb", [128, E], F32, "f")
    sb_lnbb = salloc("sb_lnbb", [128, E], F32, "f")
    FEND = cursors["f"]
    assert max(AEND, FEND, cursors["c"], cursors["s"]) <= usable, \
        (PEND, AEND, FEND, usable)

    psA = nc.alloc_psum_tensor("psA", [128, 1024], F32)
    psB = nc.alloc_psum_tensor("psB", [128, 1024], F32)
    psC = nc.alloc_psum_tensor("psC", [128, 512], F32)
    psD = nc.alloc_psum_tensor("psD", [128, 512], F32)
    psE = nc.alloc_psum_tensor("psE", [128, 512], F32)
    psF = nc.alloc_psum_tensor("psF", [128, 512], F32)

    DMAS = {"ld", "tp", "st", "gx", "gw", "gf"}
    sems = {k: Sem(nc.alloc_semaphore(k), dma=(k in DMAS)) for k in
            ["ld", "tp", "st", "gx", "gw", "gf", "pe", "sc", "ex", "pv", "bc",
             "nm", "ac", "ve", "po"]}
    sLd, sTp, sSt = sems["ld"], sems["tp"], sems["st"]
    sGx, sGw, sGf = sems["gx"], sems["gw"], sems["gf"]
    sPe, sSc, sEx, sPv = sems["pe"], sems["sc"], sems["ex"], sems["pv"]
    sBc, sNm, sAc, sVe, sPo = (sems["bc"], sems["nm"], sems["ac"], sems["ve"],
                               sems["po"])

    P = Prog()

    def dma(eng, out, in_, sem, waits=(), transpose=False):
        def fn(eh, out=out, in_=in_, transpose=transpose):
            if transpose:
                return eh.dma_start_transpose(out=out, in_=in_)
            return eh.dma_start(out=out, in_=in_)
        P.op(eng, fn, waits=waits, incs=(sem.inc(16),))
        return sem.n

    def mm(out, lhsT, rhs, start=True, stop=True, tile=None, waits=(), inc=None):
        def fn(eh, out=out, lhsT=lhsT, rhs=rhs, start=start, stop=stop, tile=tile):
            return nc.tensor.matmul(out, lhsT, rhs, start=start, stop=stop,
                                    tile_position=tile)
        incs = ()
        if inc is not None:
            incs = (inc.inc(),)
        P.op("tensor", fn, waits=waits, incs=incs)
        return inc.n if inc is not None else None

    def tr(out, in_, ident, waits=(), inc=None):
        def fn(eh, out=out, in_=in_, ident=ident):
            return nc.tensor.transpose(out, in_, ident)
        incs = (inc.inc(),) if inc is not None else ()
        P.op("tensor", fn, waits=waits, incs=incs)
        return inc.n if inc is not None else None

    def act(out, in_, func, bias=0.0, scale=1.0, accum=None, waits=(), inc=None):
        def fn(eh, out=out, in_=in_, func=func, bias=bias, scale=scale, accum=accum):
            return nc.scalar.activation(out, in_, func, bias=bias, scale=scale,
                                        accum_out=accum)
        incs = (inc.inc(),) if inc is not None else ()
        P.op("scalar", fn, waits=waits, incs=incs)
        return inc.n if inc is not None else None

    def vop(eng, name, *args, waits=(), inc=None, **kw):
        eng_obj = {"vector": nc.vector, "gpsimd": nc.gpsimd}[eng]

        def fn(eh, name=name, args=args, kw=kw):
            return getattr(eng_obj, name)(*args, **kw)
        incs = (inc.inc(),) if inc is not None else ()
        P.op(eng, fn, waits=waits, incs=incs)
        return inc.n if inc is not None else None

    def cvop(name, *args, waits=(), **kw):
        return vop("vector", name, *args,
                   waits=tuple(waits) + (sVe.at(),), inc=sVe, **kw)

    def rsqrt_rows(vrow, yrow, y2row, waits=()):
        cvop("tensor_scalar", yrow.bitcast(I32), vrow.bitcast(I32), 1,
             None, ALU.logical_shift_right, waits=waits)
        cvop("tensor_scalar", yrow.bitcast(I32), yrow.bitcast(I32),
             MAGIC, None, ALU.subtract)
        cvop("tensor_scalar", yrow.bitcast(I32), yrow.bitcast(I32), 0,
             None, ALU.bitwise_not)
        cvop("tensor_scalar", yrow.bitcast(I32), yrow.bitcast(I32), 1,
             None, ALU.add)
        for _ in range(NEWT):
            cvop("tensor_mul", y2row, yrow, yrow)
            cvop("tensor_mul", y2row, vrow, y2row)
            cvop("tensor_scalar", y2row, y2row, -0.5, 1.5, ALU.mult, ALU.add)
            cvop("tensor_mul", yrow, yrow, y2row)

    def ln_stats_vec(tsrc, sq_scr, waits_sum=(), waits_sq=()):
        cvop("reduce_sum", sb_svec[:, 0:1], tsrc, axis=AX.X, waits=waits_sum)
        a = act(sq_scr, tsrc, AF.Square, accum=sb_svec[:, 1:2], waits=waits_sq,
                inc=sAc)
        cvop("tensor_scalar", sb_svec[:, 2:3], sb_svec[:, 0:1],
             1.0 / E, None, ALU.mult, waits=(sAc.at(a),))
        cvop("tensor_scalar", sb_svec[:, 3:4], sb_svec[:, 1:2],
             1.0 / E, EPS, ALU.mult, ALU.add)
        cvop("tensor_mul", sb_svec[:, 4:5], sb_svec[:, 2:3], sb_svec[:, 2:3])
        cvop("tensor_sub", sb_svec[:, 5:6], sb_svec[:, 3:4], sb_svec[:, 4:5])
        rsqrt_rows(sb_svec[:, 5:6], sb_svec[:, 6:7], sb_svec[:, 7:8])
        cvop("tensor_mul", sb_svec[:, 9:10], sb_svec[:, 2:3], sb_svec[:, 6:7])
        v = cvop("tensor_scalar", sb_svec[:, 9:10], sb_svec[:, 9:10],
                 -1.0, None, ALU.mult)
        return v, a

    gx_kv = {}
    gx_q = {}
    gw_c = {}
    gx_kv[0] = dma("gpsimd", s_xkvbf[0:512], d_xkv[0:512], sGx)
    gw_c[1] = dma("gpsimd", s_wbf[1], d_Wk, sGw)
    gw_c[2] = dma("gpsimd", s_wbf[2], d_Wv, sGw)
    for c in range(1, 8):
        gx_kv[c] = dma("gpsimd", s_xkvbf[c * 512:(c + 1) * 512],
                       d_xkv[c * 512:(c + 1) * 512], sGx)
    gw_c[0] = dma("gpsimd", s_wbf[0], d_Wq, sGw)
    for c in range(4):
        gx_q[c] = dma("gpsimd", s_xqbf[c * 512:(c + 1) * 512],
                      d_xq[c * 512:(c + 1) * 512], sGx)
    G_X = sGx.n
    gw_c[3] = dma("gpsimd", s_wbf[3], d_Wo, sGw)
    G_GW = dma("gpsimd", sb_gw[:].rearrange("p (dc e) -> p dc e", e=NEXP),
               d_gw.rearrange("(dc p) e -> p dc e", p=128), sGw)
    G_W = sGw.n
    dma("sync", sb_gwf[:].rearrange("p (dc e) -> p dc e", e=NEXP),
        d_gw.rearrange("(dc p) e -> p dc e", p=128), sLd)
    GWF_LD = sLd.n
    g_w12 = {}
    for e in range(NEXP):
        dma("gpsimd", s_w1bf[e], d_w1[e], sGf)
        g_w12[e] = dma("gpsimd", s_w2bf[e], d_w2[e], sGf)

    LD_J = dma("sync", sb_Jf[:].rearrange("d (h e) -> d h e", e=64),
               d_J.rearrange("h d e -> d h e"), sLd)
    dma("sync", sb_lam[:], d_lam, sLd)
    vop("vector", "memset", sb_b2[:], 0.0, inc=sVe)
    for i in range(16):
        vop("vector", "memset", sb_c8t[i][:], 0.0, inc=sVe)
    B2Z_VE = sVe.n
    dma("gpsimd", sb_b2[0:NEXP, :], d_b2, sGw, waits=(sVe.at(B2Z_VE),))
    for capn, dsrc in [(c_eye1, eye1_f), (c_eyef, eye_f), (c_ones1, ones1_f),
                       (c_ones128, ones128_bf), (c_st2, st2_f), (c_sel8, sel8_f)]:
        dma("sync", capn[:], dsrc, sLd)
    LD_CONST = sLd.n

    mm(psF[0:64, 0:1], c_ones1[0:1, 0:64], sb_lam[:],
       waits=(sLd.at(LD_CONST),), inc=sPe)
    cvop("tensor_copy", sb_lamv[:], psF[0:64, 0:1], waits=(sPe.at(),))
    JL_VE = cvop("tensor_scalar", sb_J[:], sb_Jf[:], sb_lamv[:], None,
                 ALU.mult, waits=(sLd.at(LD_J),))

    def build_headvecs(dst, row_ap, dsrc):
        ldv = dma("sync", row_ap, dsrc, sLd, waits=(sPe.at(),))
        for h in range(H):
            tr(psF[0:64, h:h + 1], row_ap[0:1, h * 64:(h + 1) * 64], c_eye1[:],
               waits=((sLd.at(ldv), sVe.at()) if h == 0 else ()), inc=sPe)
        vop("vector", "tensor_copy", sb_colsc[0:64, 0:16], psF[0:64, 0:16],
            waits=(sPe.at(),), inc=sVe)
        ve_ev = sVe.n
        for h in range(H):
            mm(psE[:, h:h + 1], c_st2[:], sb_colsc[0:64, h:h + 1],
               waits=(sVe.at(ve_ev),) if h == 0 else (), inc=sPe)
        return vop("vector", "tensor_copy", dst[:, 0:16], psE[:, 0:16],
                   waits=(sPe.at(),), inc=sVe)

    BQV_VE = build_headvecs(sb_bqv, sb_rowE[:], d_bq)
    BKV_VE = build_headvecs(sb_bkv, sb_rowE[:], d_bk)

    tr_pe = {0: 0, 1: 0}
    ev_ve = {0: BKV_VE, 1: BKV_VE}
    b1bar_ve = {0: 0, 1: 0}
    items = []
    for e in range(NEXP):
        for dst, dsrc, isb1 in [(sb_b1v, d_b1, True), (sb_g1v, d_g1, False),
                                (sb_lb1v, d_lb1, False)]:
            items.append((e, dst, dsrc, isb1))
    for idx, (e, dst, dsrc, isb1) in enumerate(items):
        b = idx % 2
        rb = sb_rows if b == 0 else sb_rows2
        bank = psF if b == 0 else psE
        ldv = dma("sync", rb[0:1, :], dsrc[e:e + 1, :], sLd,
                  waits=(sPe.at(tr_pe[b]), sVe.at(b1bar_ve[b])))
        for hc in range(16):
            tr(bank[:, hc:hc + 1], rb[0:1, hc * 128:(hc + 1) * 128],
               c_eye1[:],
               waits=((sLd.at(ldv), sVe.at(ev_ve[b])) if hc == 0 else ()),
               inc=sPe)
        tr_pe[b] = sPe.n
        ev_ve[b] = vop("vector", "tensor_copy", dst[:, e * 16:(e + 1) * 16],
                       bank[:, 0:16], waits=(sPe.at(tr_pe[b]),), inc=sVe)
        if isb1:
            cvop("reduce_sum", sb_b1bar[0:1, e:e + 1], rb[0:1, :], axis=AX.X)
            b1bar_ve[b] = cvop("tensor_scalar", sb_b1bar[0:1, e:e + 1],
                               sb_b1bar[0:1, e:e + 1], 1.0 / HID, None,
                               ALU.mult)
    SETUP_VE = sVe.n
    SETUP_PE = sPe.n

    def fn_pre_exp(eh):
        return nc.scalar.preload_act_table(AF.Exp)
    P.op("scalar", fn_pre_exp)

    if LV == 0:
        of = d_out
        dma("sync", of[0:128, 0:16], sb_bqv[:], sSt, waits=(sVe.at(), sAc.at()))
        dma("sync", of[128:256, 0:16], sb_bkv[:], sSt)
        dma("sync", of[256:384, 0:128], sb_b1v[:], sSt)
        dma("sync", of[384:512, 0:128], sb_g1v[:], sSt)
        dma("sync", of[512:640, 0:128], sb_lb1v[:], sSt)
        dma("sync", of[640:704, 0:512].bitcast(BF16), sb_J[:], sSt)
        dma("sync", of[768:769, 0:8], sb_b1bar[:], sSt)
        P.op("sync", None, waits=(sSt.at(),))
        with nc.Block() as block:
            P.emit(block)
        return nc

    vp_even = lambda j, kc: (j // 2) * (32 * 65) + kc * 65
    vp_odd = lambda j, kc: 2 * 32 * 65 + (j // 2) * (32 * 128) + kc * 128

    gidg = 0
    exp_done = {}
    sc_done = {}
    pv_done = {}
    nm_tt = {}
    nm_recip = {}
    nm_evac = {}
    bc_done = {}
    last_nm = 0
    last_pass_pe = SETUP_PE

    for p in range(NPASS if LV >= 3 else 1):
        h0 = p * NHP
        fs = h0 * 64
        def wt_trans(widx, wt):
            for dc in range(8):
                w = ()
                if dc == 0:
                    w = (sGw.at(gw_c[widx]),) if p == 0 else ()
                    if p > 0 and widx == 1:
                        w += (sPe.at(last_pass_pe),)
                dma("sync", wt[:, dc * 256:(dc + 1) * 256],
                    s_wbf[widx, fs:fs + 256, dc * 128:(dc + 1) * 128], sTp,
                    waits=w, transpose=True)
        wt_trans(1, sb_wkt)
        wt_trans(2, sb_wvt)
        WT_TP = sTp.n

        pass_nm = sNm.at(last_nm)
        vop("vector", "memset", sb_vp[:], 0.0,
            waits=(pass_nm, sVe.at(SETUP_VE)), inc=sVe)
        vop("vector", "memset",
            sb_vp[:, 0:2 * 32 * 65].rearrange("p (j k c) -> p j k c", j=2, c=65)
            [:, :, :, 64:65], 1.0, inc=sVe)
        vop("vector", "memset",
            sb_vp[:, 2 * 32 * 65:].rearrange("p (j k c) -> p j k c", j=2, c=128)
            [:, :, :, 0:1], 1.0, inc=sVe)
        VP_VE = sVe.n
        bv_ld = dma("sync", sb_rowE[:], d_bv, sLd, waits=(sPe.at(),))
        mm(psF[:, 0:256], c_ones1[:], sb_rowE[0:1, fs:fs + 256],
           waits=(sLd.at(bv_ld), pass_nm, sVe.at()), inc=sPe)
        BVB_VE = vop("vector", "tensor_copy", sb_rb[:, 0:256], psF[:, 0:256],
                     waits=(sPe.at(),), inc=sVe)

        pass_pe = last_pass_pe if p > 0 else SETUP_PE
        cd_war = {id(psC): (pass_nm,), id(psD): (pass_nm,)}
        ef_war = {id(psE): (sVe.at(),), id(psF): (sVe.at(),)}
        xst_pe = [pass_pe, pass_pe]

        def stage_x(src_bf, tok0, buf, extra=()):
            for dc in range(8):
                dma("sync", sb_xst[:, buf * 4096 + dc * 512:buf * 4096 + (dc + 1) * 512],
                    src_bf[tok0:tok0 + 512, dc * 128:(dc + 1) * 128], sTp,
                    waits=(extra + (sPe.at(xst_pe[buf]),)) if dc == 0 else (),
                    transpose=True)
            return sTp.n

        for kc5 in range(8):
            buf = kc5 % 2
            extra = (sGx.at(gx_kv[kc5]),) if p == 0 else ()
            if kc5 == 0:
                extra += (sTp.at(WT_TP),)
            tpv = stage_x(s_xkvbf, kc5 * 512, buf, extra)
            xb = sb_xst[:, buf * 4096:(buf + 1) * 4096]
            for j in range(NHP if KON else 0):
                h = h0 + j
                bank = psC if (kc5 * NHP + j) % 2 == 0 else psD
                bw = tuple(cd_war.get(id(bank), ())) + \
                    ((sTp.at(tpv),) if j == 0 else ())
                for dc in range(8):
                    mm(bank[0:64, :],
                       sb_wkt[:, dc * 256 + j * 64: dc * 256 + (j + 1) * 64],
                       xb[:, dc * 512:(dc + 1) * 512],
                       start=(dc == 0), stop=(dc == 7), tile=(0, 0),
                       waits=bw if dc == 0 else ())
                for dc in range(8):
                    mm(bank[64:128, :],
                       sb_wkt[:, dc * 256 + j * 64: dc * 256 + (j + 1) * 64],
                       xb[:, dc * 512:(dc + 1) * 512],
                       start=(dc == 0), stop=(dc == 7), tile=(0, 64),
                       inc=sPe if dc == 7 else None)
                mmv = sPe.n
                if CUT < 3:
                    cd_war[id(bank)] = ((sPe.at(mmv),))
                    continue
                kslice = slice(j * SK + kc5 * 512, j * SK + (kc5 + 1) * 512)
                kv = vop("vector", "tensor_scalar", sb_kp[0:64, kslice],
                         bank[0:64, :], sb_bkv[0:64, h:h + 1], None, ALU.add,
                         waits=(sPe.at(mmv), sVe.at(BKV_VE)), inc=sVe)
                ka = act(sb_kp[64:128, kslice], bank[64:128, :], AF.Tanh,
                         bias=sb_bkv[64:128, h:h + 1],
                         waits=(sPe.at(mmv), sVe.at(kv)), inc=sAc)
                cd_war[id(bank)] = ((sVe.at(kv), sAc.at(ka)))
            for tt in range(4 if VON else 0):
                bank = psE if tt % 2 == 0 else psF
                bw = tuple(ef_war.get(id(bank), ())) + \
                    ((sVe.at(VP_VE), sVe.at(BVB_VE)) if (kc5 == 0 and tt <= 1) else ())
                for dc in range(8):
                    mm(bank[:, 0:256],
                       xb[:, dc * 512 + tt * 128: dc * 512 + (tt + 1) * 128],
                       sb_wvt[:, dc * 256:(dc + 1) * 256],
                       start=(dc == 0), stop=(dc == 7),
                       waits=bw if dc == 0 else (),
                       inc=sPe if dc == 7 else None)
                mmv = sPe.n
                kcg = kc5 * 4 + tt

                def fn_vev(eh, bank=bank, kcg=kcg):
                    vsrc = bank[:, 0:256].rearrange("p (j d) -> p j d", d=64)
                    bvb = sb_rb[:, 0:256].rearrange("p (j d) -> p j d", d=64)
                    nc.vector.tensor_tensor(
                        sb_vp[:, 0:2 * 32 * 65]
                        .rearrange("p (j k c) -> p j k c", j=2, c=65)[:, :, kcg, 0:64],
                        vsrc[:, 0::2, :], bvb[:, 0::2, :], ALU.add)
                    return nc.vector.tensor_tensor(
                        sb_vp[:, 2 * 32 * 65:]
                        .rearrange("p (j k c) -> p j k c", j=2, c=128)[:, :, kcg, 64:128],
                        vsrc[:, 1::2, :], bvb[:, 1::2, :], ALU.add)
                P.op("vector", fn_vev, waits=(sPe.at(mmv),), incs=(sVe.inc(),))
                ef_war[id(bank)] = ((sVe.at(),))
            xst_pe[buf] = sPe.n
        KV_AC = sAc.n

        wt_trans(0, sb_wqt)
        qj_war = ()
        qcount = [0]
        for qc in range(4 if CUT >= 5 else 0):
            buf = qc % 2
            tpv = stage_x(s_xqbf, qc * 512, buf,
                          (sGx.at(gx_q[qc]),) if p == 0 else ())
            xb = sb_xst[:, buf * 4096:(buf + 1) * 4096]
            for j in range(NHP):
                h = h0 + j
                bank = psC if (qc * NHP + j) % 2 == 0 else psD
                bw = tuple(cd_war.get(id(bank), ())) + \
                    ((sTp.at(tpv),) if j == 0 else ())
                for dc in range(8):
                    mm(bank[0:64, :],
                       sb_wqt[:, dc * 256 + j * 64: dc * 256 + (j + 1) * 64],
                       xb[:, dc * 512:(dc + 1) * 512],
                       start=(dc == 0), stop=(dc == 7), tile=(0, 0),
                       waits=bw if dc == 0 else (),
                       inc=sPe if dc == 7 else None)
                qmm = sPe.n
                qslice = slice(j * QT + qc * 512, j * QT + (qc + 1) * 512)
                if qlim != 111:
                    qv = vop("vector", "tensor_scalar", sb_qp[0:64, qslice],
                             bank[0:64, :], sb_bqv[0:64, h:h + 1], 0.125,
                             ALU.add, ALU.mult,
                             waits=(sPe.at(qmm), sVe.at(BQV_VE)), inc=sVe)
                    cd_war[id(bank)] = ((sVe.at(qv),))
                if CUT < 5.2:
                    continue
                qcount[0] += 1
                if qcount[0] > (qlim % 10 if qlim >= 100 else qlim):
                    continue
                qa = act(sb_qn[:], bank[0:64, :], AF.Tanh,
                         bias=sb_bqv[0:64, h:h + 1],
                         waits=(sPe.at(qmm),) + ((sVe.at(qv),) if qlim != 111 else ()),
                         inc=sAc)
                if qlim != 111:
                    cd_war[id(bank)] = (sVe.at(qv), sAc.at(qa))
                else:
                    cd_war[id(bank)] = (sAc.at(qa),)
                if CUT < 5.3:
                    continue
                qjw = qj_war + (sAc.at(qa), sVe.at(JL_VE)) + \
                    ((sVe.at(),) if (qc == 0 and j == 0) else ())
                mm(psE[64:128, :], sb_J[:, h * 64:(h + 1) * 64], sb_qn[:],
                   tile=(0, 64), waits=qjw, inc=sPe)
                qjv = sPe.n
                if CUT < 9:
                    qj_war = (sPe.at(qjv),)
                    continue
                qjev = vop("vector", "tensor_copy", sb_qp[64:128, qslice],
                           psE[64:128, :], waits=(sPe.at(qjv),), inc=sVe)
                qj_war = (sVe.at(qjev),)
            xst_pe[buf] = sPe.n
        PA_PE = sPe.n
        PA_VE = sVe.n

        if LV < 2:
            continue
        NITER = NHP * 4
        NG = 16
        base = gidg

        def it_hq(j):
            return j % NHP, j // NHP

        def emit_score(jg):
            j, g = divmod(jg, NG)
            hl, qg = it_hq(j)
            gg = base + jg
            bankp = psA if g % 2 == 0 else psB
            w = []
            if gg >= 2:
                w.append(sEx.at(exp_done[gg - 2]))
            if jg == 0:
                w += [sVe.at(PA_VE), sAc.at(KV_AC)]
            for half in range(2):
                kc = g * 2 + half
                mm(bankp[:, half * 512:(half + 1) * 512],
                   sb_kp[:, hl * SK + kc * 128: hl * SK + (kc + 1) * 128],
                   sb_qp[:, hl * QT + qg * 512: hl * QT + (qg + 1) * 512],
                   waits=tuple(w) if half == 0 else (),
                   inc=sSc if half == 1 else None)
            sc_done[gg] = sSc.n

        def emit_exp(jg):
            j, g = divmod(jg, NG)
            gg = base + jg
            bankp = psA if g % 2 == 0 else psB
            w = [sSc.at(sc_done[gg])]
            if gg >= NEXPB:
                w.append(sPv.at(pv_done[gg - NEXPB]))
            act(sb_expt[:, (gg % NEXPB) * 1024:(gg % NEXPB + 1) * 1024],
                bankp[:], AF.Exp, waits=tuple(w), inc=sEx)
            exp_done[gg] = sEx.n

        def emit_pv(jg):
            j, g = divmod(jg, NG)
            hl, qg = it_hq(j)
            gg = base + jg
            jj = p * NITER + j
            bankv = psC if j % 2 == 0 else psD
            even = (hl % 2 == 0)
            w = [sEx.at(exp_done[gg])]
            if g == 0:
                if jj >= 2:
                    w.append(sNm.at(nm_tt[jj - 2]))
                elif p > 0:
                    w.append(sNm.at(nm_tt[p * NITER - 2 + j]))
                if jj == 0:
                    w.append(sPe.at(0))
                if j <= 1 and jj >= 2:
                    pass
                if jj < 2:
                    w.append(sVe.at(PA_VE))
            eb = sb_expt[:, (gg % NEXPB) * 1024:(gg % NEXPB + 1) * 1024]
            for half in range(2):
                kc = g * 2 + half
                if even:
                    lhs = sb_vp[:, vp_even(hl, kc): vp_even(hl, kc) + 65]
                    outap = bankv[0:65, :]
                else:
                    lhs = sb_vp[:, vp_odd(hl, kc): vp_odd(hl, kc) + 128]
                    outap = bankv[:, :]
                mm(outap, lhs, eb[:, half * 512:(half + 1) * 512],
                   start=(g == 0 and half == 0), stop=(g == NG - 1 and half == 1),
                   waits=tuple(w) if half == 0 else (),
                   inc=sPv if half == 1 else None)
            pv_done[gg] = sPv.n

        def emit_recip(j):
            jj = p * NITER + j
            hl, qg = it_hq(j)
            bankv = psC if j % 2 == 0 else psD
            srow = bankv[64:65, :] if hl % 2 == 0 else bankv[0:1, :]
            w = [sPv.at(pv_done[base + j * NG + NG - 1])]
            if jj >= 1:
                w.append(sBc.at(bc_done[jj - 1]))
            nm_recip[jj] = vop("vector", "reciprocal", sb_rrow[:], srow,
                               waits=tuple(w), inc=sNm)

        def emit_bcast(j):
            jj = p * NITER + j
            w = [sNm.at(nm_recip[jj])]
            if jj >= 1:
                w.append(sNm.at(nm_evac[jj - 1]))
            mm(psE[:, :], c_ones1[:], sb_rrow[:], waits=tuple(w), inc=sBc)
            bc_done[jj] = sBc.n

        def emit_norm(j):
            jj = p * NITER + j
            hl, qg = it_hq(j)
            h = h0 + hl
            bankv = psC if j % 2 == 0 else psD
            even = (hl % 2 == 0)
            nm_evac[jj] = vop("vector", "tensor_copy", sb_rb[:, 0:512],
                              psE[:, :], waits=(sBc.at(bc_done[jj]),), inc=sNm)
            r0 = 0 if even else 64
            fc = h // 2
            nm_tt[jj] = vop(
                "vector", "tensor_tensor",
                sb_att[r0:r0 + 64, fc * QT + qg * 512: fc * QT + (qg + 1) * 512],
                bankv[r0:r0 + 64, :], sb_rb[r0:r0 + 64, 0:512], ALU.mult, inc=sNm)

        emit_score(0)
        for jg in range(NITER * NG):
            j, g = divmod(jg, NG)
            if jg + 1 < NITER * NG:
                emit_score(jg + 1)
            emit_exp(jg)
            if g == 2 and j >= 1:
                emit_bcast(j - 1)
                emit_norm(j - 1)
            emit_pv(jg)
            if g == NG - 1:
                emit_recip(j)
        emit_bcast(NITER - 1)
        emit_norm(NITER - 1)
        gidg += NITER * NG
        last_nm = sNm.n
        last_pass_pe = sPe.n

    ATT_NM = sNm.n
    ATT_EX = sEx.n

    if LV <= 3:
        ob = d_out.bitcast(BF16).rearrange("(a p) n -> p a n", p=128)
        if LV == 1:
            dma("sync", ob[:, 0:8, :],
                sb_kp[:].rearrange("p (a n) -> p a n", n=2048), sSt,
                waits=(sVe.at(),) if qlim == 105 else (sVe.at(), sAc.at()))
            dma("sync", ob[:, 8:12, :],
                sb_qp[:].rearrange("p (a n) -> p a n", n=2048), sSt)
            dma("sync", ob[:, 12:14, :],
                sb_vp[:, 0:4096].rearrange("p (a n) -> p a n", n=2048), sSt)
        else:
            dma("sync", ob[:, 0:8, :],
                sb_att[:].rearrange("p (a n) -> p a n", n=2048), sSt,
                waits=(sNm.at(),))
        P.op("sync", None, waits=(sSt.at(),))
        with nc.Block() as block:
            P.emit(block)
        return nc

    for dc in range(8):
        dma("sync", sb_wot[:, dc * 1024:(dc + 1) * 1024],
            s_wbf[3, :, dc * 128:(dc + 1) * 128], sTp,
            waits=(sGw.at(), sNm.at(ATT_NM)) if dc == 0 else (), transpose=True)
    WOT_TP = sTp.n
    bo_ld = dma("sync", sb_rowE[:], d_bo, sLd, waits=(sPe.at(),))
    mm(psC[:, :], c_ones1[:], sb_rowE[0:1, 0:512],
       waits=(sLd.at(bo_ld), sNm.at(ATT_NM), sVe.at(), sAc.at()), inc=sPe)
    mm(psD[:, :], c_ones1[:], sb_rowE[0:1, 512:1024], inc=sPe)
    bo_pe = sPe.n
    vop("vector", "tensor_copy", sb_bob[:, 0:512], psC[:, :],
        waits=(sPe.at(bo_pe),))
    BOB_VE = vop("vector", "tensor_copy", sb_bob[:, 512:1024], psD[:, :], inc=sVe)

    ab_war = {id(psA): (), id(psB): ()}
    xq_war = {0: (), 1: ()}
    xf_st = {}
    prev_xfa = 0
    for tt in range(16):
        bank = psA if tt % 2 == 0 else psB
        xbuf = sb_xqt[:, (tt % 2) * E:(tt % 2 + 1) * E]
        xld = dma("sync", xbuf, d_xq[tt * 128:(tt + 1) * 128, :], sTp,
                  waits=xq_war[tt % 2])
        w = list(ab_war[id(bank)])
        if tt == 0:
            w += [sTp.at(WOT_TP), sEx.at(ATT_EX), sNm.at(ATT_NM)]
        for ic in range(8):
            for half in range(2):
                mm(bank[:, half * 512:(half + 1) * 512],
                   sb_att[:, ic * QT + tt * 128: ic * QT + (tt + 1) * 128],
                   sb_wot[:, ic * 1024 + half * 512: ic * 1024 + (half + 1) * 512],
                   start=(ic == 0), stop=(ic == 7),
                   waits=tuple(w) if (ic == 0 and half == 0) else (),
                   inc=sPe if (ic == 7 and half == 1) else None)
        opv = sPe.n
        vop("vector", "tensor_tensor", sb_t0a[:], bank[:], xbuf, ALU.add,
            waits=(sPe.at(opv), sTp.at(xld), sVe.at(BOB_VE)))
        t0v = vop("vector", "tensor_tensor", sb_t0a[:], sb_t0a[:], sb_bob[:],
                  ALU.add, inc=sVe)
        ab_war[id(bank)] = (sVe.at(t0v),)
        xq_war[tt % 2] = (sVe.at(t0v),)
        rv, _ = ln_stats_vec(sb_t0a[:], sb_scr[:],
                             waits_sum=(sVe.at(t0v), sAc.at(prev_xfa)),
                             waits_sq=(sVe.at(t0v),))
        xfa = act(sb_xf[:], sb_t0a[:], AF.Identity, bias=sb_svec[:, 9:10],
                  scale=sb_svec[:, 6:7], waits=(sVe.at(rv),), inc=sAc)
        prev_xfa = xfa
        xfbv = vop("vector", "tensor_copy", sb_xfb[:], sb_xf[:],
                   waits=(sAc.at(xfa),), inc=sVe)
        xlov = cvop("tensor_sub", sb_xlo[:], sb_xf[:], sb_xfb[:],
                    waits=(sVe.at(xfbv),))
        dma("scalar", s_xf32[tt * 128:(tt + 1) * 128, :], sb_xf[:], sSt,
            waits=(sAc.at(xfa),))
        dma("scalar", s_xfbf[tt * 128:(tt + 1) * 128, :], sb_xfb[:],
            sSt, waits=(sVe.at(xfbv),))
        xf_st[tt] = dma("scalar", s_xflo[tt * 128:(tt + 1) * 128, :], sb_xlo[:],
                        sSt, waits=(sVe.at(xlov),))
    if LV == 4:
        for tt in range(16):
            dma("sync", d_out[tt * 128:(tt + 1) * 128, :],
                s_xf32[tt * 128:(tt + 1) * 128, :], sSt, waits=(sSt.at(),))
        P.op("sync", None, waits=(sSt.at(),))
        with nc.Block() as block:
            P.emit(block)
        return nc
    PC_ST = sSt.n
    PC_VE = sVe.n
    PC_AC = sAc.n

    for dc in range(8):
        dma("sync", sb_xfm[:, dc * QT:(dc + 1) * QT],
            s_xfbf[:, dc * 128:(dc + 1) * 128], sTp,
            waits=(sSt.at(PC_ST),) if dc == 0 else (), transpose=True)
    for dc in range(8):
        dma("sync", sb_xfmlo[:, dc * QT:(dc + 1) * QT],
            s_xflo[:, dc * 128:(dc + 1) * 128], sTp, transpose=True)
    XFM_TP = sTp.n

    GWLO_VE = vop("vector", "tensor_sub", sb_gwlo[:], sb_gwf[:], sb_gw[:],
                  waits=(sLd.at(GWF_LD), sGw.at(G_GW)), inc=sVe)
    g = sb_gz
    gz_war = ()
    for tt in range(16):
        w = list(gz_war)
        if tt == 0:
            w += [sTp.at(XFM_TP), sGw.at(), sVe.at(PC_VE), sAc.at(PC_AC),
                  sVe.at(GWLO_VE)]
        for gi, (xsrc, wsrc) in enumerate([(sb_xfm, sb_gw), (sb_xfm, sb_gwlo),
                                           (sb_xfmlo, sb_gw)]):
            for dc in range(8):
                mm(psF[:, 0:NEXP],
                   xsrc[:, dc * QT + tt * 128: dc * QT + (tt + 1) * 128],
                   wsrc[:, dc * NEXP:(dc + 1) * NEXP],
                   start=(gi == 0 and dc == 0), stop=(gi == 2 and dc == 7),
                   waits=tuple(w) if (gi == 0 and dc == 0) else (),
                   inc=sPe if (gi == 2 and dc == 7) else None)
        zv = sPe.n
        cvop("reduce_max", g[:, 8:9], psF[:, 0:NEXP], axis=AX.X,
             waits=(sPe.at(zv),))
        m1v = cvop("tensor_scalar", g[:, 9:10], g[:, 8:9], -1.0, None, ALU.mult)
        v8 = sb_scr[:, 0:8]
        m8 = sb_scr[:, 8:16]
        cvop("tensor_scalar", v8, psF[:, 0:NEXP], g[:, 8:9], None, ALU.is_ge)
        cvop("scalar_tensor_tensor", v8, v8, -1e9, psF[:, 0:NEXP],
             ALU.mult, ALU.add)
        cvop("reduce_max", g[:, 10:11], v8, axis=AX.X)
        ge2 = cvop("tensor_scalar", m8, psF[:, 0:NEXP], g[:, 10:11], None,
                   ALU.is_ge)
        eza = act(g[:, 0:8], psF[:, 0:NEXP], AF.Exp, bias=g[:, 9:10],
                  waits=(sVe.at(ge2),), inc=sAc)
        cvop("tensor_tensor", v8, m8, g[:, 0:8], ALU.mult,
             waits=(sAc.at(eza),))
        cvop("reduce_sum", g[:, 11:12], v8, axis=AX.X)
        cvop("reciprocal", g[:, 12:13], g[:, 11:12])
        cv = cvop("tensor_scalar", g[:, 0:8], v8, g[:, 12:13], None, ALU.mult)
        trv = tr(psE[0:8, 0:128], g[:, 0:8], c_eyef[:], waits=(sVe.at(cv),),
                 inc=sPe)
        c8v = vop("vector", "tensor_copy", sb_c8[:, tt * 128:(tt + 1) * 128],
                  psE[0:8, 0:128], waits=(sPe.at(trv),), inc=sVe)
        gz_war = (sVe.at(c8v), sAc.at(eza))
    for i in range(16):
        vop("vector", "tensor_copy", sb_c8t[i][0:NEXP, :],
            sb_c8[:, i * 128:(i + 1) * 128], inc=sVe)
    GATES_VE = sVe.n
    GATES_AC = sAc.n
    GATES_PE = sPe.n
    if LV == 5:
        of = d_out.rearrange("(a b) n -> a (b n)", b=2)
        dma("sync", of[0:8, :], sb_c8[:], sSt, waits=(sVe.at(GATES_VE),))
        P.op("sync", None, waits=(sSt.at(),))
        with nc.Block() as block:
            P.emit(block)
        return nc

    def fn_pre_gelu(eh):
        return nc.scalar.preload_act_table(AF.Gelu)
    P.op("scalar", fn_pre_gelu)

    w1_ld = {}
    w2_ld = {}
    dma("sync", sb_w1[:].rearrange("p (dc hh) -> p dc hh", hh=HID),
        s_w1bf[0].rearrange("(dc p) hh -> p dc hh", p=128), sTp,
        waits=(sGf.at(), sTp.at(XFM_TP), sPe.at(GATES_PE)))
    w1_ld[0] = sTp.n
    dma("sync", sb_w2[:].rearrange("p (hc dd) -> p hc dd", dd=E),
        s_w2bf[0].rearrange("(hc p) dd -> p hc dd", p=128), sTp,
        waits=(sGf.at(), sPe.at(GATES_PE)))
    w2_ld[0] = sTp.n

    lng_ld = dma("sync", sb_rowE[:], d_lng, sLd, waits=(sPe.at(),))
    mm(psC[:, :], c_ones1[:], sb_rowE[0:1, 0:512],
       waits=(sLd.at(lng_ld), sVe.at(BOB_VE)), inc=sPe)
    mm(psD[:, :], c_ones1[:], sb_rowE[0:1, 512:1024], inc=sPe)
    gpe = sPe.n
    vop("vector", "tensor_copy", sb_lngb[:, 0:512], psC[:, :],
        waits=(sPe.at(gpe),))
    gve = vop("vector", "tensor_copy", sb_lngb[:, 512:1024], psD[:, :],
              inc=sVe)
    lnb_ld = dma("sync", sb_rowE[:], d_lnb, sLd, waits=(sPe.at(gpe),))
    mm(psC[:, :], c_ones1[:], sb_rowE[0:1, 0:512],
       waits=(sLd.at(lnb_ld), sVe.at(gve)), inc=sPe)
    mm(psD[:, :], c_ones1[:], sb_rowE[0:1, 512:1024], inc=sPe)
    gpe2 = sPe.n
    vop("vector", "tensor_copy", sb_lnbb[:, 0:512], psC[:, :],
        waits=(sPe.at(gpe2),))
    lnb_ve = vop("vector", "tensor_copy", sb_lnbb[:, 512:1024],
                 psD[:, :], inc=sVe)

    NSLAB = NQ_RUN * NEXP * NTS
    hsq_war3 = {0: (), 1: (), 2: ()}
    h_war = {hc: () for hc in range(16)}
    hp_war = {hc: () for hc in range(16)}
    t1_war = {0: (), 1: ()}
    t2_war = {0: (), 1: ()}
    gg_war = {0: (), 1: ()}
    fbc_war = (sAc.at(GATES_AC), sVe.at(GATES_VE))
    cb_war = (sVe.at(GATES_VE),)
    y_war = {id(psA): (sVe.at(PC_VE),), id(psB): (sVe.at(PC_VE),)}
    cd_war = {id(psC): (sVe.at(lnb_ve),), id(psD): (sVe.at(lnb_ve),)}
    fxq_war = {0: xq_war[0], 1: xq_war[1]}
    out_war = ()
    w1bar_done = {}
    prev_outa = 0
    t0f_war = ()
    srd_prev = GATES_VE
    fin_pending = []
    fin_gens = []

    def fin_pump(k):
        while k > 0 and fin_gens:
            try:
                next(fin_gens[0])
                k -= 1
            except StopIteration:
                fin_gens.pop(0)

    rstd_v = {}
    stats_pe_s = {}
    w2_done = {}
    pa_cnt = {}
    mub_mm = 0
    rst_mm = 0
    mub_war = ()
    cbread_po = ()

    for i in range(NSLAB + 1):
        sP, sN = i, i - 1
        have1 = sP < NSLAB
        have2 = sN >= 0
        if have1:
            qp_, ep = sP // (NEXP * NTS), (sP // NTS) % NEXP
            tsp = sP % NTS
            itp = qp_ * NEXP + ep
            tok0p = qp_ * QTR + tsp * TSF
            if qp_ == 0 and tsp == 0:
                for dc in range(8):
                    cvop("reduce_sum",
                         sb_w1bar[:, ep * 8 + dc: ep * 8 + dc + 1],
                         sb_w1[:, dc * HID:(dc + 1) * HID], axis=AX.X,
                         waits=(sTp.at(w1_ld[itp]),) if dc == 0 else ())
                cvop("tensor_scalar", sb_w1bar[:, ep * 8:(ep + 1) * 8],
                     sb_w1bar[:, ep * 8:(ep + 1) * 8], 1.0 / HID, None,
                     ALU.mult)
                w1bar_done[ep] = cvop("tensor_copy",
                                      sb_w1barb[:, ep * 8:(ep + 1) * 8],
                                      sb_w1bar[:, ep * 8:(ep + 1) * 8])
        if have2:
            qn, en = sN // (NEXP * NTS), (sN // NTS) % NEXP
            tsn = sN % NTS
            itn = qn * NEXP + en
            tok0n = qn * QTR + tsn * TSF
            mm(psF[:, 0:TSF], c_ones1[:], sb_mrow[:],
               waits=(sVe.at(rstd_v[sN]),) + tuple(fbc_war), inc=sPe)
            mub_mm = sPe.n
            mub_v = vop("vector", "tensor_copy", sb_mub[:], psF[:, 0:TSF],
                        waits=(sPe.at(mub_mm),) + tuple(mub_war), inc=sVe)
            mm(psF[:, 0:TSF], c_ones1[:], sb_yrow[:],
               waits=(sVe.at(mub_v),), inc=sPe)
            rst_mm = sPe.n
            rb_v = vop("vector", "tensor_copy", sb_rstb[:], psF[:, 0:TSF],
                       waits=(sPe.at(rst_mm),), inc=sVe)
            mm(psF[:, 0:TSF], c_sel8[:, en * 128:(en + 1) * 128],
               sb_c8[:, tok0n:tok0n + TSF],
               waits=(sVe.at(rb_v),) + tuple(cb_war), inc=sPe)
            cb_v = vop("vector", "tensor_copy", sb_cb[:], psF[:, 0:TSF],
                       waits=(sPe.at(),) + tuple(cbread_po), inc=sVe)
            fbc_war = (sVe.at(cb_v),)
            cb_war = (sVe.at(cb_v),)
        sq_ac = {}
        pcm = {}
        w1mm_last = 0
        for hc in range(16):
            if have2:
                pA = vop("vector", "tensor_tensor",
                         sb_t1[:, (hc % 2) * TSF:(hc % 2 + 1) * TSF],
                         sb_h[:, hc * TSF:(hc + 1) * TSF], sb_mub[:],
                         ALU.subtract,
                         waits=((sVe.at(mub_v),) if hc == 0 else ()) +
                         tuple(t1_war[hc % 2]), inc=sVe)
                pa_cnt[hc] = pA
                pB = vop("vector", "scalar_tensor_tensor",
                         sb_t2[:, (hc % 2) * TSF:(hc % 2 + 1) * TSF],
                         sb_t1[:, (hc % 2) * TSF:(hc % 2 + 1) * TSF],
                         sb_g1v[:, en * 16 + hc: en * 16 + hc + 1],
                         sb_rstb[:], ALU.mult, ALU.mult,
                         waits=((sVe.at(rb_v),) if hc == 0 else ()) +
                         tuple(t2_war[hc % 2]), inc=sVe)
                t1_war[hc % 2] = (sVe.at(pB),)
                pG = act(sb_gg[:, (hc % 2) * TSF:(hc % 2 + 1) * TSF],
                         sb_t2[:, (hc % 2) * TSF:(hc % 2 + 1) * TSF],
                         AF.Gelu,
                         bias=sb_lb1v[:, en * 16 + hc: en * 16 + hc + 1],
                         waits=(sVe.at(pB),) + tuple(gg_war[hc % 2]), inc=sAc)
                t2_war[hc % 2] = (sAc.at(pG),)
                pC = vop("gpsimd", "tensor_tensor",
                         sb_hp[:, hc * TSF:(hc + 1) * TSF],
                         sb_gg[:, (hc % 2) * TSF:(hc % 2 + 1) * TSF],
                         sb_cb[:], ALU.mult,
                         waits=(sAc.at(pG),) +
                         ((sVe.at(cb_v),) if hc == 0 else ()) +
                         tuple(hp_war[hc]), inc=sPo)
                gg_war[hc % 2] = (sPo.at(pC),)
                pcm[hc] = pC
                if hc == 15:
                    mub_war = (sVe.at(pA),)
                    cbread_po = (sPo.at(pC),)
            if have1:
                bank = psC if hc % 2 == 0 else psD
                w = list(cd_war[id(bank)])
                if hc == 0:
                    w.append(sTp.at(w1_ld[itp]))
                    if sP == 0:
                        w.append(sTp.at(XFM_TP))
                for dc in range(8):
                    mm(bank[:, 0:TSF],
                       sb_w1[:, dc * HID + hc * 128: dc * HID + (hc + 1) * 128],
                       sb_xfm[:, dc * QT + tok0p: dc * QT + tok0p + TSF],
                       start=(dc == 0), stop=(dc == 7),
                       waits=tuple(w) if dc == 0 else (),
                       inc=sPe if dc == 7 else None)
                w1mm_last = sPe.n
                hev = vop("vector", "tensor_scalar",
                          sb_h[:, hc * TSF:(hc + 1) * TSF], bank[:, 0:TSF],
                          sb_b1v[:, ep * 16 + hc: ep * 16 + hc + 1], None,
                          ALU.add,
                          waits=(sPe.at(w1mm_last),) +
                          (() if have2 else tuple(h_war[hc])), inc=sVe)
                cd_war[id(bank)] = (sVe.at(hev),)
                sqd = act(sb_hsq[:, (hc % 3) * TSF:(hc % 3 + 1) * TSF],
                          sb_h[:, hc * TSF:(hc + 1) * TSF], AF.Square,
                          waits=(sVe.at(hev),) + tuple(hsq_war3[hc % 3]),
                          inc=sAc)
                sq_ac[hc] = sqd
                if hc == 0:
                    for dc in range(8):
                        mm(psF[0:1, 0:TSF],
                           sb_w1barb[:, ep * 8 + dc: ep * 8 + dc + 1],
                           sb_xfm[:, dc * QT + tok0p: dc * QT + tok0p + TSF],
                           start=(dc == 0), stop=(dc == 7), tile=(0, 0),
                           waits=((sVe.at(w1bar_done[ep]),) + tuple(fbc_war))
                           if dc == 0 else (),
                           inc=sPe if dc == 7 else None)
                hs = hc - 2
                if hs >= 0:
                    ssw = (sAc.at(sq_ac[hs]),)
                    if hs == 0:
                        ssw += (sVe.at(srd_prev),)
                    mm(psE[32:33, 0:TSF], c_ones128[:],
                       sb_hsq[:, (hs % 3) * TSF:(hs % 3 + 1) * TSF],
                       start=(hs == 0), stop=(hs == 15), tile=(0, 32),
                       waits=ssw, inc=sPe)
                    hsq_war3[hs % 3] = (sPe.at(sPe.n),)
            if hc % 4 == 3:
                fin_pump(2)
        if have1:
            for hs in (14, 15):
                mm(psE[32:33, 0:TSF], c_ones128[:],
                   sb_hsq[:, (hs % 3) * TSF:(hs % 3 + 1) * TSF],
                   start=False, stop=(hs == 15),
                   tile=(0, 32), waits=(sAc.at(sq_ac[hs]),), inc=sPe)
                hsq_war3[hs % 3] = (sPe.at(sPe.n),)
            stats_pe_s[sP] = sPe.n
            cvop("tensor_scalar", sb_mrow[:], psF[0:1, 0:TSF],
                 sb_b1bar[0:1, ep:ep + 1], None, ALU.add,
                 waits=(sPe.at(stats_pe_s[sP]), sPe.at(mub_mm)))
            srd_prev = cvop("tensor_scalar", sb_srow[:],
                            psE[32:33, 0:TSF], 1.0 / HID, EPS, ALU.mult,
                            ALU.add)
            cvop("tensor_mul", sb_vrow[:], sb_mrow[:], sb_mrow[:])
            cvop("tensor_sub", sb_vrow[:], sb_srow[:], sb_vrow[:])
            rsqrt_rows(sb_vrow[:], sb_yrow[:], sb_y2row[:],
                       waits=(sPe.at(rst_mm),))
            rstd_v[sP] = sVe.n
        if have2:
            for tt in range(2):
                gt4 = tsn * 2 + tt
                ybank = psA if (qn * 4 + gt4) % 2 == 0 else psB
                w = list(y_war[id(ybank)])
                w.append(sTp.at(w2_ld[itn]))
                w.append(sPo.at(pcm[15]))
                for hc in range(16):
                    whc = tuple(w)
                    w = []
                    for half in range(2):
                        mm(ybank[:, half * 512:(half + 1) * 512],
                           sb_hp[:, hc * TSF + tt * 128:
                                 hc * TSF + tt * 128 + 128],
                           sb_w2[:, hc * E + half * 512:
                                 hc * E + (half + 1) * 512],
                           start=(hc == 0), stop=(hc == 15),
                           waits=whc if half == 0 else (),
                           inc=sPe if (hc == 15 and half == 1) else None)
                w2mm = sPe.n
                ydst = sb_y[:, gt4 * E:(gt4 + 1) * E]
                if en == 0:
                    yv = vop("vector", "tensor_copy", ydst, ybank[:],
                             waits=(sPe.at(w2mm),), inc=sVe)
                else:
                    yv = vop("vector", "tensor_tensor", ydst, ybank[:],
                             ydst, ALU.add, waits=(sPe.at(w2mm),), inc=sVe)
                y_war[id(ybank)] = (sVe.at(yv),)
            w2_done[sN] = sPe.n
            for hc in range(16):
                hp_war[hc] = (sPe.at(w2_done[sN]),)
            if tsn == 1 and itn + 1 < NQ_RUN * NEXP and (itn + 1) not in w2_ld:
                ne2 = (itn + 1) % NEXP
                dma("sync", sb_w2[:].rearrange("p (hc dd) -> p hc dd", dd=E),
                    s_w2bf[ne2].rearrange("(hc p) dd -> p hc dd", p=128), sTp,
                    waits=(sGf.at(), sPe.at(w2_done[sN])))
                w2_ld[itn + 1] = sTp.n
        if have1 and tsp == 1 and itp + 1 < NQ_RUN * NEXP:
            ne1 = (itp + 1) % NEXP
            wld = [sGf.at(), sPe.at(w1mm_last)]
            if qp_ == 0:
                wld.append(sVe.at(w1bar_done[ep]))
            dma("sync", sb_w1[:].rearrange("p (dc hh) -> p dc hh", hh=HID),
                s_w1bf[ne1].rearrange("(dc p) hh -> p dc hh", p=128), sTp,
                waits=tuple(wld))
            w1_ld[itp + 1] = sTp.n

        if have2 and en == NEXP - 1 and tsn == NTS - 1:
            fin_pending += [(qn, 0), (qn, 1), (qn, 2), (qn, 3)]

        def emit_fin_tile(q, tt):
            nonlocal prev_outa, t0f_war, out_war
            if True:
                gt = q * 4 + tt
                ybank = psA if gt % 2 == 0 else psB
                xbuf = sb_xqt[:, (gt % 2) * E:(gt % 2 + 1) * E]
                xld = dma("sync", xbuf, s_xf32[gt * 128:(gt + 1) * 128, :],
                          sTp, waits=(sSt.at(xf_st[gt]),) +
                          tuple(fxq_war[gt % 2]))
                w = list(y_war[id(ybank)]) + [sVe.at(GATES_VE), sGw.at()]
                mm(ybank[:, 0:512], sb_c8t[gt][:],
                   sb_b2[:, 0:512], waits=tuple(w), inc=sPe)
                mm(ybank[:, 512:1024], sb_c8t[gt][:],
                   sb_b2[:, 512:1024], inc=sPe)
                b2pe = sPe.n
                ydst = sb_y[:, tt * E:(tt + 1) * E]
                vop("vector", "tensor_tensor", sb_t0f[:], ybank[:], ydst,
                    ALU.add,
                    waits=(sPe.at(b2pe), sVe.at(lnb_ve)) + tuple(t0f_war))
                t0v = vop("vector", "tensor_tensor", sb_t0f[:], sb_t0f[:],
                          xbuf, ALU.add, waits=(sTp.at(xld),), inc=sVe)
                y_war[id(ybank)] = (sVe.at(t0v),)
                yield
                rv, _ = ln_stats_vec(sb_t0f[:], sb_scrf[:],
                                     waits_sum=(sVe.at(t0v),
                                                sAc.at(prev_outa)),
                                     waits_sq=(sVe.at(t0v),))
                t2a = act(sb_scrf[:], sb_t0f[:], AF.Identity,
                          bias=sb_svec[:, 9:10], scale=sb_svec[:, 6:7],
                          waits=(sVe.at(rv),), inc=sAc)
                prev_outa = t2a
                yield
                vop("gpsimd", "tensor_tensor", sb_t0f[:], sb_scrf[:],
                    sb_lngb[:], ALU.mult, waits=(sAc.at(t2a),))
                vop("gpsimd", "tensor_tensor", sb_t0f[:], sb_t0f[:],
                    sb_lnbb[:], ALU.add)
                t5v = vop("gpsimd", "tensor_tensor", sb_t0f[:], sb_t0f[:],
                          xbuf, ALU.add, inc=sPo)
                fxq_war[gt % 2] = (sPo.at(t5v),)
                sqg = vop("gpsimd", "tensor_tensor", sb_scrf[:], sb_t0f[:],
                          sb_t0f[:], ALU.mult, inc=sPo)
                cvop("reduce_sum", sb_svec[:, 1:2], sb_scrf[:], axis=AX.X,
                     waits=(sPo.at(sqg),))
                cvop("reduce_sum", sb_svec[:, 0:1], sb_t0f[:], axis=AX.X,
                     waits=(sPo.at(t5v),))
                cvop("tensor_scalar", sb_svec[:, 2:3], sb_svec[:, 0:1],
                     1.0 / E, None, ALU.mult)
                cvop("tensor_scalar", sb_svec[:, 3:4], sb_svec[:, 1:2],
                     1.0 / E, EPS, ALU.mult, ALU.add)
                cvop("tensor_mul", sb_svec[:, 4:5], sb_svec[:, 2:3],
                     sb_svec[:, 2:3])
                cvop("tensor_sub", sb_svec[:, 5:6], sb_svec[:, 3:4],
                     sb_svec[:, 4:5])
                rsqrt_rows(sb_svec[:, 5:6], sb_svec[:, 6:7], sb_svec[:, 7:8])
                cvop("tensor_mul", sb_svec[:, 9:10], sb_svec[:, 2:3],
                     sb_svec[:, 6:7])
                rv2 = cvop("tensor_scalar", sb_svec[:, 9:10],
                           sb_svec[:, 9:10], -1.0, None, ALU.mult)
                yield
                outa = act(sb_outt[:], sb_t0f[:], AF.Identity,
                           bias=sb_svec[:, 9:10], scale=sb_svec[:, 6:7],
                           waits=(sVe.at(rv2),) + tuple(out_war), inc=sAc)
                prev_outa = outa
                t0f_war = (sAc.at(outa),)
                dma("scalar", d_out[gt * 128:(gt + 1) * 128, :], sb_outt[:],
                    sSt, waits=(sAc.at(outa),))
                out_war = (sSt.at(),)

        while fin_pending:
            fin_gens.append(emit_fin_tile(*fin_pending.pop(0)))

    fin_pump(1 << 30)

    P.op("sync", None, waits=(sSt.at(),))

    with nc.Block() as block:
        P.emit(block)
    return nc


GATESVE = None


def kernel(**inputs):
    x = np.asarray(inputs["L0_attn_x"])
    com = {
        "Wq": np.ascontiguousarray(inputs["L0_attn_Wq"], np.float32),
        "bq": np.ascontiguousarray(inputs["L0_attn_bq"], np.float32).reshape(1, E),
        "Wk": np.ascontiguousarray(inputs["L0_attn_Wk"], np.float32),
        "bk": np.ascontiguousarray(inputs["L0_attn_bk"], np.float32).reshape(1, E),
        "Wv": np.ascontiguousarray(inputs["L0_attn_Wv"], np.float32),
        "bv": np.ascontiguousarray(inputs["L0_attn_bv"], np.float32).reshape(1, E),
        "Wo": np.ascontiguousarray(inputs["L0_attn_Wo"], np.float32),
        "bo": np.ascontiguousarray(inputs["L0_attn_bo"], np.float32).reshape(1, E),
        "J": np.ascontiguousarray(inputs["L0_attn_J_attn"], np.float32),
        "lam": np.asarray(inputs["L0_attn_lambda_attn"], np.float32).reshape(1, 1),
        "gw": np.ascontiguousarray(inputs["L0_ffn_gate_w"], np.float32),
        "w1": np.ascontiguousarray(inputs["L0_ffn_w1"], np.float32),
        "b1": np.ascontiguousarray(inputs["L0_ffn_b1"], np.float32),
        "g1": np.ascontiguousarray(inputs["L0_ffn_ln1_g"], np.float32),
        "lb1": np.ascontiguousarray(inputs["L0_ffn_ln1_b"], np.float32),
        "w2": np.ascontiguousarray(inputs["L0_ffn_w2"], np.float32),
        "b2": np.ascontiguousarray(inputs["L0_ffn_b2"], np.float32),
        "lng": np.ascontiguousarray(inputs["L0_ffn_ln_g"], np.float32).reshape(1, E),
        "lnb": np.ascontiguousarray(inputs["L0_ffn_ln_b"], np.float32).reshape(1, E),
    }
    in_maps = []
    for c in range(8):
        b = c // 2
        q0 = (c % 2) * QT
        m = dict(com)
        m["xq"] = np.ascontiguousarray(x[b, q0:q0 + QT], np.float32)
        m["xkv"] = np.ascontiguousarray(x[b], np.float32)
        in_maps.append(m)
    nc = build_program()
    res = run_bass_kernel_spmd(nc, in_maps, core_ids=list(range(8)))
    out = np.empty((B, S, E), np.float32)
    for c in range(8):
        b = c // 2
        q0 = (c % 2) * QT
        out[b, q0:q0 + QT] = np.asarray(res.results[c]["out"])
    return out


# version 10
# speedup vs baseline: 1.0140x; 1.0140x over previous
import numpy as np
import ml_dtypes
import concourse.bass as bass
import concourse.mybir as mybir
from concourse.bass_utils import run_bass_kernel_spmd

F32 = mybir.dt.float32
BF16 = mybir.dt.bfloat16
I32 = mybir.dt.int32
AF = mybir.ActivationFunctionType
AX = mybir.AxisListType
ALU = mybir.AluOpType

B, S, E, H, D = 4, 4096, 1024, 16, 64
QT = 2048
SK = 4096
NHP = 4
NPASS = H // NHP
HID = 2048
NEXP = 8
EPS = 1e-5
MAGIC = 0x5F3759DF
NEWT = 3
TSF = 256
NTS = 2
QTR = 512
NQTR = QT // QTR
NEXPB = 3

ENGS = ("sync", "tensor", "scalar", "vector", "gpsimd")


class Sem:
    __slots__ = ("h", "n", "dma")

    def __init__(self, h, dma=False):
        self.h = h
        self.n = 0
        self.dma = dma

    def inc(self, k=1):
        self.n += k
        return (self, k)

    def at(self, v=None):
        if self.dma:
            return (self, self.n)
        return (self, self.n if v is None else v)


class Prog:
    def __init__(self):
        self.q = {e: [] for e in ENGS}

    def op(self, eng, fn=None, waits=(), incs=()):
        self.q[eng].append((fn, tuple(waits), tuple(incs)))

    def emit(self, block):
        for eng in ENGS:
            entries = self.q[eng]

            def make(entries=entries):
                def run(eh):
                    for fn, waits, incs in entries:
                        for (s, v) in waits:
                            if v > 0:
                                eh.wait_ge(s.h, v)
                        ins = fn(eh) if fn is not None else None
                        for (s, k) in incs:
                            assert ins is not None
                            ins.then_inc(s.h, k)
                return run
            getattr(block, eng)(make())


def build_program(upto="all", cut="all", qlim=99):
    CUT = {"stage": 1, "kmm": 2, "kev": 3, "v": 4, "q": 5, "q1": 5.1, "q2": 5.2,
           "q3": 5.3, "qo": 5.25, "all": 9}[cut]
    CUT = float(CUT)
    if cut == "qo":
        KON, VON = False, False
    else:
        KON, VON = CUT >= 2, CUT >= 4
    LV = {"su": 0, "pa": 1, "pb": 2, "att": 3, "pc": 4, "pg": 5,
          "f1": 6, "f2": 7, "f3": 8, "all": 9}[upto]
    NQ_RUN = {6: 1, 7: 1, 8: 1}.get(LV, NQTR)
    NE_RUN = {6: 1}.get(LV, NEXP)
    FIN_RUN = LV not in (6, 7)
    nc = bass.Bass("TRN2", target_bir_lowering=False, detect_race_conditions=False)

    d_xq = nc.dram_tensor("xq", [QT, E], F32, kind="ExternalInput").ap()
    d_xkv = nc.dram_tensor("xkv", [SK, E], F32, kind="ExternalInput").ap()
    d_Wq = nc.dram_tensor("Wq", [E, E], F32, kind="ExternalInput").ap()
    d_Wk = nc.dram_tensor("Wk", [E, E], F32, kind="ExternalInput").ap()
    d_Wv = nc.dram_tensor("Wv", [E, E], F32, kind="ExternalInput").ap()
    d_Wo = nc.dram_tensor("Wo", [E, E], F32, kind="ExternalInput").ap()
    d_bq = nc.dram_tensor("bq", [1, E], F32, kind="ExternalInput").ap()
    d_bk = nc.dram_tensor("bk", [1, E], F32, kind="ExternalInput").ap()
    d_bv = nc.dram_tensor("bv", [1, E], F32, kind="ExternalInput").ap()
    d_bo = nc.dram_tensor("bo", [1, E], F32, kind="ExternalInput").ap()
    d_J = nc.dram_tensor("J", [H, D, D], F32, kind="ExternalInput").ap()
    d_lam = nc.dram_tensor("lam", [1, 1], F32, kind="ExternalInput").ap()
    d_gw = nc.dram_tensor("gw", [E, NEXP], F32, kind="ExternalInput").ap()
    d_w1 = nc.dram_tensor("w1", [NEXP, E, HID], F32, kind="ExternalInput").ap()
    d_b1 = nc.dram_tensor("b1", [NEXP, HID], F32, kind="ExternalInput").ap()
    d_g1 = nc.dram_tensor("g1", [NEXP, HID], F32, kind="ExternalInput").ap()
    d_lb1 = nc.dram_tensor("lb1", [NEXP, HID], F32, kind="ExternalInput").ap()
    d_w2 = nc.dram_tensor("w2", [NEXP, HID, E], F32, kind="ExternalInput").ap()
    d_b2 = nc.dram_tensor("b2", [NEXP, E], F32, kind="ExternalInput").ap()
    d_lng = nc.dram_tensor("lng", [1, E], F32, kind="ExternalInput").ap()
    d_lnb = nc.dram_tensor("lnb", [1, E], F32, kind="ExternalInput").ap()
    d_out = nc.dram_tensor("out", [QT, E], F32, kind="ExternalOutput").ap()

    s_xkvbf = nc.dram_tensor("s_xkvbf", [SK, E], BF16, kind="Internal").ap()
    s_xqbf = nc.dram_tensor("s_xqbf", [QT, E], BF16, kind="Internal").ap()
    s_wbf = nc.dram_tensor("s_wbf", [4, E, E], BF16, kind="Internal").ap()
    s_w1bf = nc.dram_tensor("s_w1bf", [NEXP, E, HID], BF16, kind="Internal").ap()
    s_w2bf = nc.dram_tensor("s_w2bf", [NEXP, HID, E], BF16, kind="Internal").ap()
    s_xf32 = nc.dram_tensor("s_xf32", [QT, E], F32, kind="Internal").ap()
    s_xfbf = nc.dram_tensor("s_xfbf", [QT, E], BF16, kind="Internal").ap()
    s_xflo = nc.dram_tensor("s_xflo", [QT, E], BF16, kind="Internal").ap()

    eye1_f = nc.inline_tensor(np.eye(1, dtype=np.float32), name="eye1_f").ap()
    eye_f = nc.inline_tensor(np.eye(128, dtype=np.float32), name="eye_f").ap()
    ones1_f = nc.inline_tensor(np.ones((1, 128), np.float32), name="ones1_f").ap()
    ones128_bf = nc.inline_tensor(np.ones((128, 1), ml_dtypes.bfloat16),
                                  name="ones128_bf").ap()
    st2_np = np.zeros((64, 128), np.float32)
    st2_np[np.arange(64), np.arange(64)] = 1.0
    st2_np[np.arange(64), 64 + np.arange(64)] = 1.0
    st2_f = nc.inline_tensor(st2_np, name="st2_f").ap()
    sel_np = np.zeros((NEXP, NEXP * 128), np.float32)
    for e in range(NEXP):
        sel_np[e, e * 128:(e + 1) * 128] = 1.0
    sel8_f = nc.inline_tensor(sel_np, name="sel8_f").ap()

    sb_base = (nc.sbuf_base + 31) // 32 * 32
    usable = nc.sbuf_top - sb_base
    cursors = {}

    def region(name, start):
        cursors[name] = start

    def salloc(name, shape, dtype, reg):
        esz = {F32: 4, BF16: 2, I32: 4}[dtype]
        nb = int(np.prod(shape[1:])) * esz if len(shape) > 1 else esz
        nb = (nb + 31) // 32 * 32
        off = cursors[reg]
        assert off + nb <= usable, f"SBUF overflow: {name} reg={reg} end={off+nb} usable={usable}"
        cursors[reg] = off + nb
        return nc.alloc_sbuf_tensor_at(name, list(shape), dtype, offset=sb_base + off)

    region("p", 0)
    c_eye1 = salloc("c_eye1", [1, 1], F32, "p")
    c_eyef = salloc("c_eyef", [128, 128], F32, "p")
    c_ones1 = salloc("c_ones1", [1, 128], F32, "p")
    c_ones128 = salloc("c_ones128", [128, 1], BF16, "p")
    c_st2 = salloc("c_st2", [64, 128], F32, "p")
    c_sel8 = salloc("c_sel8", [NEXP, NEXP * 128], F32, "p")
    sb_J = salloc("sb_J", [64, H * 64], BF16, "p")
    sb_lam = salloc("sb_lam", [1, 1], F32, "p")
    sb_lamv = salloc("sb_lamv", [64, 1], F32, "p")
    sb_bqv = salloc("sb_bqv", [128, H], F32, "p")
    sb_bkv = salloc("sb_bkv", [128, H], F32, "p")
    sb_rowE = salloc("sb_rowE", [1, E], F32, "p")
    sb_b1v = salloc("sb_b1v", [128, NEXP * 16], F32, "p")
    sb_g1v = salloc("sb_g1v", [128, NEXP * 16], F32, "p")
    sb_lb1v = salloc("sb_lb1v", [128, NEXP * 16], F32, "p")
    sb_w1bar = salloc("sb_w1bar", [128, NEXP * 8], F32, "p")
    sb_w1barb = salloc("sb_w1barb", [128, NEXP * 8], BF16, "p")
    sb_b1bar = salloc("sb_b1bar", [1, NEXP], F32, "p")
    sb_b2 = salloc("sb_b2", [128, E], BF16, "p")
    sb_c8t = [salloc(f"sb_c8t{i}", [128, 128], BF16, "p") for i in range(16)]
    sb_gw = salloc("sb_gw", [128, 8 * NEXP], BF16, "p")
    sb_gwf = salloc("sb_gwf", [128, 8 * NEXP], F32, "p")
    sb_gwlo = salloc("sb_gwlo", [128, 8 * NEXP], BF16, "p")
    sb_c8 = salloc("sb_c8", [NEXP, QT], F32, "p")
    sb_colsc = salloc("sb_colsc", [128, 16], F32, "p")
    sb_svec = salloc("sb_svec", [128, 12], F32, "p")
    sb_xqt = salloc("sb_xqt", [128, 2 * E], F32, "p")
    PEND = cursors["p"]

    region("a", PEND)
    sb_wqt = salloc("sb_wqt", [128, 8 * 256], BF16, "a")
    sb_wkt = salloc("sb_wkt", [128, 8 * 256], BF16, "a")
    sb_wvt = salloc("sb_wvt", [128, 8 * 256], BF16, "a")
    sb_kp = salloc("sb_kp", [128, NHP * SK], BF16, "a")
    sb_vp = salloc("sb_vp", [128, 2 * 32 * 65 + 2 * 32 * 128], BF16, "a")
    sb_qp = salloc("sb_qp", [128, NHP * QT], BF16, "a")
    sb_xst = salloc("sb_xst", [128, 2 * 8 * 512], BF16, "a")
    sb_expt = salloc("sb_expt", [128, NEXPB * 1024], BF16, "a")
    sb_qn = salloc("sb_qn", [64, 512], BF16, "a")
    sb_rrow = salloc("sb_rrow", [1, 512], F32, "a")
    sb_rb = salloc("sb_rb", [128, 512], F32, "a")
    sb_att = salloc("sb_att", [128, 8 * QT], BF16, "a")
    sb_bob = salloc("sb_bob", [128, E], F32, "a")
    sb_xf = salloc("sb_xf", [128, E], F32, "a")
    sb_xfb = salloc("sb_xfb", [128, E], BF16, "a")
    sb_xlo = salloc("sb_xlo", [128, E], BF16, "a")
    sb_t0a = salloc("sb_t0a", [128, E], F32, "a")
    sb_scr = salloc("sb_scr", [128, E], F32, "a")
    AEND = cursors["a"]

    region("c", PEND + 3 * (8 * 256 * 2))
    sb_wot = salloc("sb_wot", [128, 8 * 1024], BF16, "c")

    region("f", PEND)
    sb_xfm = salloc("sb_xfm", [128, 8 * QT], BF16, "f")
    w1_off = cursors["f"]
    sb_w1 = salloc("sb_w1", [128, 8 * HID], BF16, "f")
    region("g2", w1_off)
    sb_xfmlo = salloc("sb_xfmlo", [128, 8 * QT], BF16, "g2")
    sb_w2 = salloc("sb_w2", [128, 16 * E], BF16, "f")
    y_off = cursors["f"]
    sb_y = salloc("sb_y", [128, 4 * E], F32, "f")
    region("s", y_off)
    sb_rows = salloc("sb_rows", [1, HID], F32, "s")
    sb_Jf = salloc("sb_Jf", [64, H * 64], F32, "s")
    sb_rows2 = salloc("sb_rows2", [1, HID], F32, "s")
    sb_h = salloc("sb_h", [128, 16 * TSF], F32, "f")
    sb_hsq = salloc("sb_hsq", [128, 3 * TSF], BF16, "f")
    sb_hp = salloc("sb_hp", [128, 16 * TSF], BF16, "f")
    sb_t1 = salloc("sb_t1", [128, 2 * TSF], F32, "f")
    sb_t2 = salloc("sb_t2", [128, 2 * TSF], F32, "f")
    sb_gg = salloc("sb_gg", [128, 2 * TSF], F32, "f")
    sb_mub = salloc("sb_mub", [128, TSF], F32, "f")
    sb_rstb = salloc("sb_rstb", [128, TSF], F32, "f")
    sb_cb = salloc("sb_cb", [128, TSF], F32, "f")
    sb_mrow = salloc("sb_mrow", [1, TSF], F32, "f")
    sb_srow = salloc("sb_srow", [1, TSF], F32, "f")
    sb_vrow = salloc("sb_vrow", [1, TSF], F32, "f")
    sb_yrow = salloc("sb_yrow", [1, TSF], F32, "f")
    sb_y2row = salloc("sb_y2row", [1, TSF], F32, "f")
    sb_gz = salloc("sb_gz", [128, 16], F32, "f")
    sb_t0f = salloc("sb_t0f", [128, E], F32, "f")
    sb_scrf = salloc("sb_scrf", [128, E], F32, "f")
    sb_outt = salloc("sb_outt", [128, E], F32, "f")
    sb_lngb = salloc("sb_lngb", [128, E], F32, "f")
    sb_lnbb = salloc("sb_lnbb", [128, E], F32, "f")
    FEND = cursors["f"]
    assert max(AEND, FEND, cursors["c"], cursors["s"]) <= usable, \
        (PEND, AEND, FEND, usable)

    psA = nc.alloc_psum_tensor("psA", [128, 1024], F32)
    psB = nc.alloc_psum_tensor("psB", [128, 1024], F32)
    psC = nc.alloc_psum_tensor("psC", [128, 512], F32)
    psD = nc.alloc_psum_tensor("psD", [128, 512], F32)
    psE = nc.alloc_psum_tensor("psE", [128, 512], F32)
    psF = nc.alloc_psum_tensor("psF", [128, 512], F32)

    DMAS = {"ld", "tp", "st", "gx", "gw", "gf"}
    sems = {k: Sem(nc.alloc_semaphore(k), dma=(k in DMAS)) for k in
            ["ld", "tp", "st", "gx", "gw", "gf", "pe", "sc", "ex", "pv", "bc",
             "nm", "ac", "ve", "po"]}
    sLd, sTp, sSt = sems["ld"], sems["tp"], sems["st"]
    sGx, sGw, sGf = sems["gx"], sems["gw"], sems["gf"]
    sPe, sSc, sEx, sPv = sems["pe"], sems["sc"], sems["ex"], sems["pv"]
    sBc, sNm, sAc, sVe, sPo = (sems["bc"], sems["nm"], sems["ac"], sems["ve"],
                               sems["po"])

    P = Prog()

    def dma(eng, out, in_, sem, waits=(), transpose=False):
        def fn(eh, out=out, in_=in_, transpose=transpose):
            if transpose:
                return eh.dma_start_transpose(out=out, in_=in_)
            return eh.dma_start(out=out, in_=in_)
        P.op(eng, fn, waits=waits, incs=(sem.inc(16),))
        return sem.n

    def mm(out, lhsT, rhs, start=True, stop=True, tile=None, waits=(), inc=None):
        def fn(eh, out=out, lhsT=lhsT, rhs=rhs, start=start, stop=stop, tile=tile):
            return nc.tensor.matmul(out, lhsT, rhs, start=start, stop=stop,
                                    tile_position=tile)
        incs = ()
        if inc is not None:
            incs = (inc.inc(),)
        P.op("tensor", fn, waits=waits, incs=incs)
        return inc.n if inc is not None else None

    def tr(out, in_, ident, waits=(), inc=None):
        def fn(eh, out=out, in_=in_, ident=ident):
            return nc.tensor.transpose(out, in_, ident)
        incs = (inc.inc(),) if inc is not None else ()
        P.op("tensor", fn, waits=waits, incs=incs)
        return inc.n if inc is not None else None

    def act(out, in_, func, bias=0.0, scale=1.0, accum=None, waits=(), inc=None):
        def fn(eh, out=out, in_=in_, func=func, bias=bias, scale=scale, accum=accum):
            return nc.scalar.activation(out, in_, func, bias=bias, scale=scale,
                                        accum_out=accum)
        incs = (inc.inc(),) if inc is not None else ()
        P.op("scalar", fn, waits=waits, incs=incs)
        return inc.n if inc is not None else None

    def vop(eng, name, *args, waits=(), inc=None, **kw):
        eng_obj = {"vector": nc.vector, "gpsimd": nc.gpsimd}[eng]

        def fn(eh, name=name, args=args, kw=kw):
            return getattr(eng_obj, name)(*args, **kw)
        incs = (inc.inc(),) if inc is not None else ()
        P.op(eng, fn, waits=waits, incs=incs)
        return inc.n if inc is not None else None

    def cvop(name, *args, waits=(), **kw):
        return vop("vector", name, *args,
                   waits=tuple(waits) + (sVe.at(),), inc=sVe, **kw)

    def rsqrt_rows(vrow, yrow, y2row, waits=()):
        cvop("tensor_scalar", yrow.bitcast(I32), vrow.bitcast(I32), 1,
             None, ALU.logical_shift_right, waits=waits)
        cvop("tensor_scalar", yrow.bitcast(I32), yrow.bitcast(I32),
             MAGIC, None, ALU.subtract)
        cvop("tensor_scalar", yrow.bitcast(I32), yrow.bitcast(I32), 0,
             None, ALU.bitwise_not)
        cvop("tensor_scalar", yrow.bitcast(I32), yrow.bitcast(I32), 1,
             None, ALU.add)
        for _ in range(NEWT):
            cvop("tensor_mul", y2row, yrow, yrow)
            cvop("tensor_mul", y2row, vrow, y2row)
            cvop("tensor_scalar", y2row, y2row, -0.5, 1.5, ALU.mult, ALU.add)
            cvop("tensor_mul", yrow, yrow, y2row)

    def ln_stats_vec(tsrc, sq_scr, waits_sum=(), waits_sq=()):
        cvop("reduce_sum", sb_svec[:, 0:1], tsrc, axis=AX.X, waits=waits_sum)
        a = act(sq_scr, tsrc, AF.Square, accum=sb_svec[:, 1:2], waits=waits_sq,
                inc=sAc)
        cvop("tensor_scalar", sb_svec[:, 2:3], sb_svec[:, 0:1],
             1.0 / E, None, ALU.mult, waits=(sAc.at(a),))
        cvop("tensor_scalar", sb_svec[:, 3:4], sb_svec[:, 1:2],
             1.0 / E, EPS, ALU.mult, ALU.add)
        cvop("tensor_mul", sb_svec[:, 4:5], sb_svec[:, 2:3], sb_svec[:, 2:3])
        cvop("tensor_sub", sb_svec[:, 5:6], sb_svec[:, 3:4], sb_svec[:, 4:5])
        rsqrt_rows(sb_svec[:, 5:6], sb_svec[:, 6:7], sb_svec[:, 7:8])
        cvop("tensor_mul", sb_svec[:, 9:10], sb_svec[:, 2:3], sb_svec[:, 6:7])
        v = cvop("tensor_scalar", sb_svec[:, 9:10], sb_svec[:, 9:10],
                 -1.0, None, ALU.mult)
        return v, a

    gx_kv = {}
    gx_q = {}
    gw_c = {}
    gx_kv[0] = dma("gpsimd", s_xkvbf[0:512], d_xkv[0:512], sGx)
    gw_c[1] = dma("gpsimd", s_wbf[1], d_Wk, sGw)
    gw_c[2] = dma("gpsimd", s_wbf[2], d_Wv, sGw)
    for c in range(1, 8):
        gx_kv[c] = dma("gpsimd", s_xkvbf[c * 512:(c + 1) * 512],
                       d_xkv[c * 512:(c + 1) * 512], sGx)
    gw_c[0] = dma("gpsimd", s_wbf[0], d_Wq, sGw)
    for c in range(4):
        gx_q[c] = dma("gpsimd", s_xqbf[c * 512:(c + 1) * 512],
                      d_xq[c * 512:(c + 1) * 512], sGx)
    G_X = sGx.n
    gw_c[3] = dma("gpsimd", s_wbf[3], d_Wo, sGw)
    G_GW = dma("gpsimd", sb_gw[:].rearrange("p (dc e) -> p dc e", e=NEXP),
               d_gw.rearrange("(dc p) e -> p dc e", p=128), sGw)
    G_W = sGw.n
    dma("sync", sb_gwf[:].rearrange("p (dc e) -> p dc e", e=NEXP),
        d_gw.rearrange("(dc p) e -> p dc e", p=128), sLd)
    GWF_LD = sLd.n
    g_w12 = {}
    for e in range(NEXP):
        dma("gpsimd", s_w1bf[e], d_w1[e], sGf)
        g_w12[e] = dma("gpsimd", s_w2bf[e], d_w2[e], sGf)

    LD_J = dma("sync", sb_Jf[:].rearrange("d (h e) -> d h e", e=64),
               d_J.rearrange("h d e -> d h e"), sLd)
    dma("sync", sb_lam[:], d_lam, sLd)
    vop("vector", "memset", sb_b2[:], 0.0, inc=sVe)
    for i in range(16):
        vop("vector", "memset", sb_c8t[i][:], 0.0, inc=sVe)
    B2Z_VE = sVe.n
    dma("gpsimd", sb_b2[0:NEXP, :], d_b2, sGw, waits=(sVe.at(B2Z_VE),))
    for capn, dsrc in [(c_eye1, eye1_f), (c_eyef, eye_f), (c_ones1, ones1_f),
                       (c_ones128, ones128_bf), (c_st2, st2_f), (c_sel8, sel8_f)]:
        dma("sync", capn[:], dsrc, sLd)
    LD_CONST = sLd.n

    mm(psF[0:64, 0:1], c_ones1[0:1, 0:64], sb_lam[:],
       waits=(sLd.at(LD_CONST),), inc=sPe)
    cvop("tensor_copy", sb_lamv[:], psF[0:64, 0:1], waits=(sPe.at(),))
    JL_VE = cvop("tensor_scalar", sb_J[:], sb_Jf[:], sb_lamv[:], None,
                 ALU.mult, waits=(sLd.at(LD_J),))

    def build_headvecs(dst, row_ap, dsrc):
        ldv = dma("sync", row_ap, dsrc, sLd, waits=(sPe.at(),))
        for h in range(H):
            tr(psF[0:64, h:h + 1], row_ap[0:1, h * 64:(h + 1) * 64], c_eye1[:],
               waits=((sLd.at(ldv), sVe.at()) if h == 0 else ()), inc=sPe)
        vop("vector", "tensor_copy", sb_colsc[0:64, 0:16], psF[0:64, 0:16],
            waits=(sPe.at(),), inc=sVe)
        ve_ev = sVe.n
        for h in range(H):
            mm(psE[:, h:h + 1], c_st2[:], sb_colsc[0:64, h:h + 1],
               waits=(sVe.at(ve_ev),) if h == 0 else (), inc=sPe)
        return vop("vector", "tensor_copy", dst[:, 0:16], psE[:, 0:16],
                   waits=(sPe.at(),), inc=sVe)

    BQV_VE = build_headvecs(sb_bqv, sb_rowE[:], d_bq)
    BKV_VE = build_headvecs(sb_bkv, sb_rowE[:], d_bk)

    tr_pe = {0: 0, 1: 0}
    ev_ve = {0: BKV_VE, 1: BKV_VE}
    b1bar_ve = {0: 0, 1: 0}
    items = []
    for e in range(NEXP):
        for dst, dsrc, isb1 in [(sb_b1v, d_b1, True), (sb_g1v, d_g1, False),
                                (sb_lb1v, d_lb1, False)]:
            items.append((e, dst, dsrc, isb1))
    for idx, (e, dst, dsrc, isb1) in enumerate(items):
        b = idx % 2
        rb = sb_rows if b == 0 else sb_rows2
        bank = psF if b == 0 else psE
        ldv = dma("sync", rb[0:1, :], dsrc[e:e + 1, :], sLd,
                  waits=(sPe.at(tr_pe[b]), sVe.at(b1bar_ve[b])))
        for hc in range(16):
            tr(bank[:, hc:hc + 1], rb[0:1, hc * 128:(hc + 1) * 128],
               c_eye1[:],
               waits=((sLd.at(ldv), sVe.at(ev_ve[b])) if hc == 0 else ()),
               inc=sPe)
        tr_pe[b] = sPe.n
        ev_ve[b] = vop("vector", "tensor_copy", dst[:, e * 16:(e + 1) * 16],
                       bank[:, 0:16], waits=(sPe.at(tr_pe[b]),), inc=sVe)
        if isb1:
            cvop("reduce_sum", sb_b1bar[0:1, e:e + 1], rb[0:1, :], axis=AX.X)
            b1bar_ve[b] = cvop("tensor_scalar", sb_b1bar[0:1, e:e + 1],
                               sb_b1bar[0:1, e:e + 1], 1.0 / HID, None,
                               ALU.mult)
    SETUP_VE = sVe.n
    SETUP_PE = sPe.n

    def fn_pre_exp(eh):
        return nc.scalar.preload_act_table(AF.Exp)
    P.op("scalar", fn_pre_exp)

    if LV == 0:
        of = d_out
        dma("sync", of[0:128, 0:16], sb_bqv[:], sSt, waits=(sVe.at(), sAc.at()))
        dma("sync", of[128:256, 0:16], sb_bkv[:], sSt)
        dma("sync", of[256:384, 0:128], sb_b1v[:], sSt)
        dma("sync", of[384:512, 0:128], sb_g1v[:], sSt)
        dma("sync", of[512:640, 0:128], sb_lb1v[:], sSt)
        dma("sync", of[640:704, 0:512].bitcast(BF16), sb_J[:], sSt)
        dma("sync", of[768:769, 0:8], sb_b1bar[:], sSt)
        P.op("sync", None, waits=(sSt.at(),))
        with nc.Block() as block:
            P.emit(block)
        return nc

    vp_even = lambda j, kc: (j // 2) * (32 * 65) + kc * 65
    vp_odd = lambda j, kc: 2 * 32 * 65 + (j // 2) * (32 * 128) + kc * 128

    gidg = 0
    exp_done = {}
    sc_done = {}
    pv_done = {}
    nm_tt = {}
    nm_recip = {}
    nm_evac = {}
    bc_done = {}
    last_nm = 0
    last_pass_pe = SETUP_PE

    for p in range(NPASS if LV >= 3 else 1):
        h0 = p * NHP
        fs = h0 * 64
        def wt_trans(widx, wt):
            for dc in range(8):
                w = ()
                if dc == 0:
                    w = (sGw.at(gw_c[widx]),) if p == 0 else ()
                    if p > 0 and widx == 1:
                        w += (sPe.at(last_pass_pe),)
                dma("sync", wt[:, dc * 256:(dc + 1) * 256],
                    s_wbf[widx, fs:fs + 256, dc * 128:(dc + 1) * 128], sTp,
                    waits=w, transpose=True)
        wt_trans(1, sb_wkt)
        wt_trans(2, sb_wvt)
        WT_TP = sTp.n

        pass_nm = sNm.at(last_nm)
        vop("vector", "memset", sb_vp[:], 0.0,
            waits=(pass_nm, sVe.at(SETUP_VE)), inc=sVe)
        vop("vector", "memset",
            sb_vp[:, 0:2 * 32 * 65].rearrange("p (j k c) -> p j k c", j=2, c=65)
            [:, :, :, 64:65], 1.0, inc=sVe)
        vop("vector", "memset",
            sb_vp[:, 2 * 32 * 65:].rearrange("p (j k c) -> p j k c", j=2, c=128)
            [:, :, :, 0:1], 1.0, inc=sVe)
        VP_VE = sVe.n
        bv_ld = dma("sync", sb_rowE[:], d_bv, sLd, waits=(sPe.at(),))
        mm(psF[:, 0:256], c_ones1[:], sb_rowE[0:1, fs:fs + 256],
           waits=(sLd.at(bv_ld), pass_nm, sVe.at()), inc=sPe)
        BVB_VE = vop("vector", "tensor_copy", sb_rb[:, 0:256], psF[:, 0:256],
                     waits=(sPe.at(),), inc=sVe)

        pass_pe = last_pass_pe if p > 0 else SETUP_PE
        cd_war = {id(psC): (pass_nm,), id(psD): (pass_nm,)}
        ef_war = {id(psE): (sVe.at(),), id(psF): (sVe.at(),)}
        xst_pe = [pass_pe, pass_pe]

        def stage_x(src_bf, tok0, buf, extra=()):
            for dc in range(8):
                dma("sync", sb_xst[:, buf * 4096 + dc * 512:buf * 4096 + (dc + 1) * 512],
                    src_bf[tok0:tok0 + 512, dc * 128:(dc + 1) * 128], sTp,
                    waits=(extra + (sPe.at(xst_pe[buf]),)) if dc == 0 else (),
                    transpose=True)
            return sTp.n

        for kc5 in range(8):
            buf = kc5 % 2
            extra = (sGx.at(gx_kv[kc5]),) if p == 0 else ()
            if kc5 == 0:
                extra += (sTp.at(WT_TP),)
            tpv = stage_x(s_xkvbf, kc5 * 512, buf, extra)
            xb = sb_xst[:, buf * 4096:(buf + 1) * 4096]
            for j in range(NHP if KON else 0):
                h = h0 + j
                bank = psC if (kc5 * NHP + j) % 2 == 0 else psD
                bw = tuple(cd_war.get(id(bank), ())) + \
                    ((sTp.at(tpv),) if j == 0 else ())
                for dc in range(8):
                    mm(bank[0:64, :],
                       sb_wkt[:, dc * 256 + j * 64: dc * 256 + (j + 1) * 64],
                       xb[:, dc * 512:(dc + 1) * 512],
                       start=(dc == 0), stop=(dc == 7), tile=(0, 0),
                       waits=bw if dc == 0 else ())
                for dc in range(8):
                    mm(bank[64:128, :],
                       sb_wkt[:, dc * 256 + j * 64: dc * 256 + (j + 1) * 64],
                       xb[:, dc * 512:(dc + 1) * 512],
                       start=(dc == 0), stop=(dc == 7), tile=(0, 64),
                       inc=sPe if dc == 7 else None)
                mmv = sPe.n
                if CUT < 3:
                    cd_war[id(bank)] = ((sPe.at(mmv),))
                    continue
                kslice = slice(j * SK + kc5 * 512, j * SK + (kc5 + 1) * 512)
                kv = vop("vector", "tensor_scalar", sb_kp[0:64, kslice],
                         bank[0:64, :], sb_bkv[0:64, h:h + 1], None, ALU.add,
                         waits=(sPe.at(mmv), sVe.at(BKV_VE)), inc=sVe)
                ka = act(sb_kp[64:128, kslice], bank[64:128, :], AF.Tanh,
                         bias=sb_bkv[64:128, h:h + 1],
                         waits=(sPe.at(mmv), sVe.at(kv)), inc=sAc)
                cd_war[id(bank)] = ((sVe.at(kv), sAc.at(ka)))
            for tt in range(4 if VON else 0):
                bank = psE if tt % 2 == 0 else psF
                bw = tuple(ef_war.get(id(bank), ())) + \
                    ((sVe.at(VP_VE), sVe.at(BVB_VE)) if (kc5 == 0 and tt <= 1) else ())
                for dc in range(8):
                    mm(bank[:, 0:256],
                       xb[:, dc * 512 + tt * 128: dc * 512 + (tt + 1) * 128],
                       sb_wvt[:, dc * 256:(dc + 1) * 256],
                       start=(dc == 0), stop=(dc == 7),
                       waits=bw if dc == 0 else (),
                       inc=sPe if dc == 7 else None)
                mmv = sPe.n
                kcg = kc5 * 4 + tt

                def fn_vev(eh, bank=bank, kcg=kcg):
                    vsrc = bank[:, 0:256].rearrange("p (j d) -> p j d", d=64)
                    bvb = sb_rb[:, 0:256].rearrange("p (j d) -> p j d", d=64)
                    nc.vector.tensor_tensor(
                        sb_vp[:, 0:2 * 32 * 65]
                        .rearrange("p (j k c) -> p j k c", j=2, c=65)[:, :, kcg, 0:64],
                        vsrc[:, 0::2, :], bvb[:, 0::2, :], ALU.add)
                    return nc.vector.tensor_tensor(
                        sb_vp[:, 2 * 32 * 65:]
                        .rearrange("p (j k c) -> p j k c", j=2, c=128)[:, :, kcg, 64:128],
                        vsrc[:, 1::2, :], bvb[:, 1::2, :], ALU.add)
                P.op("vector", fn_vev, waits=(sPe.at(mmv),), incs=(sVe.inc(),))
                ef_war[id(bank)] = ((sVe.at(),))
            xst_pe[buf] = sPe.n
        KV_AC = sAc.n

        wt_trans(0, sb_wqt)
        qj_war = ()
        qcount = [0]
        for qc in range(4 if CUT >= 5 else 0):
            buf = qc % 2
            tpv = stage_x(s_xqbf, qc * 512, buf,
                          (sGx.at(gx_q[qc]),) if p == 0 else ())
            xb = sb_xst[:, buf * 4096:(buf + 1) * 4096]
            for j in range(NHP):
                h = h0 + j
                bank = psC if (qc * NHP + j) % 2 == 0 else psD
                bw = tuple(cd_war.get(id(bank), ())) + \
                    ((sTp.at(tpv),) if j == 0 else ())
                for dc in range(8):
                    mm(bank[0:64, :],
                       sb_wqt[:, dc * 256 + j * 64: dc * 256 + (j + 1) * 64],
                       xb[:, dc * 512:(dc + 1) * 512],
                       start=(dc == 0), stop=(dc == 7), tile=(0, 0),
                       waits=bw if dc == 0 else (),
                       inc=sPe if dc == 7 else None)
                qmm = sPe.n
                qslice = slice(j * QT + qc * 512, j * QT + (qc + 1) * 512)
                if qlim != 111:
                    qv = vop("vector", "tensor_scalar", sb_qp[0:64, qslice],
                             bank[0:64, :], sb_bqv[0:64, h:h + 1], 0.125,
                             ALU.add, ALU.mult,
                             waits=(sPe.at(qmm), sVe.at(BQV_VE)), inc=sVe)
                    cd_war[id(bank)] = ((sVe.at(qv),))
                if CUT < 5.2:
                    continue
                qcount[0] += 1
                if qcount[0] > (qlim % 10 if qlim >= 100 else qlim):
                    continue
                qa = act(sb_qn[:], bank[0:64, :], AF.Tanh,
                         bias=sb_bqv[0:64, h:h + 1],
                         waits=(sPe.at(qmm),) + ((sVe.at(qv),) if qlim != 111 else ()),
                         inc=sAc)
                if qlim != 111:
                    cd_war[id(bank)] = (sVe.at(qv), sAc.at(qa))
                else:
                    cd_war[id(bank)] = (sAc.at(qa),)
                if CUT < 5.3:
                    continue
                qjw = qj_war + (sAc.at(qa), sVe.at(JL_VE)) + \
                    ((sVe.at(),) if (qc == 0 and j == 0) else ())
                mm(psE[64:128, :], sb_J[:, h * 64:(h + 1) * 64], sb_qn[:],
                   tile=(0, 64), waits=qjw, inc=sPe)
                qjv = sPe.n
                if CUT < 9:
                    qj_war = (sPe.at(qjv),)
                    continue
                qjev = vop("vector", "tensor_copy", sb_qp[64:128, qslice],
                           psE[64:128, :], waits=(sPe.at(qjv),), inc=sVe)
                qj_war = (sVe.at(qjev),)
            xst_pe[buf] = sPe.n
        PA_PE = sPe.n
        PA_VE = sVe.n

        if LV < 2:
            continue
        NITER = NHP * 4
        NG = 16
        base = gidg

        def it_hq(j):
            return j % NHP, j // NHP

        def emit_score(jg):
            j, g = divmod(jg, NG)
            hl, qg = it_hq(j)
            gg = base + jg
            bankp = psA if g % 2 == 0 else psB
            w = []
            if gg >= 2:
                w.append(sEx.at(exp_done[gg - 2]))
            if jg == 0:
                w += [sVe.at(PA_VE), sAc.at(KV_AC)]
            for half in range(2):
                kc = g * 2 + half
                mm(bankp[:, half * 512:(half + 1) * 512],
                   sb_kp[:, hl * SK + kc * 128: hl * SK + (kc + 1) * 128],
                   sb_qp[:, hl * QT + qg * 512: hl * QT + (qg + 1) * 512],
                   waits=tuple(w) if half == 0 else (),
                   inc=sSc if half == 1 else None)
            sc_done[gg] = sSc.n

        def emit_exp(jg):
            j, g = divmod(jg, NG)
            gg = base + jg
            bankp = psA if g % 2 == 0 else psB
            w = [sSc.at(sc_done[gg])]
            if gg >= NEXPB:
                w.append(sPv.at(pv_done[gg - NEXPB]))
            act(sb_expt[:, (gg % NEXPB) * 1024:(gg % NEXPB + 1) * 1024],
                bankp[:], AF.Exp, waits=tuple(w), inc=sEx)
            exp_done[gg] = sEx.n

        def emit_pv(jg):
            j, g = divmod(jg, NG)
            hl, qg = it_hq(j)
            gg = base + jg
            jj = p * NITER + j
            bankv = psC if j % 2 == 0 else psD
            even = (hl % 2 == 0)
            w = [sEx.at(exp_done[gg])]
            if g == 0:
                if jj >= 2:
                    w.append(sNm.at(nm_tt[jj - 2]))
                elif p > 0:
                    w.append(sNm.at(nm_tt[p * NITER - 2 + j]))
                if jj == 0:
                    w.append(sPe.at(0))
                if j <= 1 and jj >= 2:
                    pass
                if jj < 2:
                    w.append(sVe.at(PA_VE))
            eb = sb_expt[:, (gg % NEXPB) * 1024:(gg % NEXPB + 1) * 1024]
            for half in range(2):
                kc = g * 2 + half
                if even:
                    lhs = sb_vp[:, vp_even(hl, kc): vp_even(hl, kc) + 65]
                    outap = bankv[0:65, :]
                else:
                    lhs = sb_vp[:, vp_odd(hl, kc): vp_odd(hl, kc) + 128]
                    outap = bankv[:, :]
                mm(outap, lhs, eb[:, half * 512:(half + 1) * 512],
                   start=(g == 0 and half == 0), stop=(g == NG - 1 and half == 1),
                   waits=tuple(w) if half == 0 else (),
                   inc=sPv if half == 1 else None)
            pv_done[gg] = sPv.n

        def emit_recip(j):
            jj = p * NITER + j
            hl, qg = it_hq(j)
            bankv = psC if j % 2 == 0 else psD
            srow = bankv[64:65, :] if hl % 2 == 0 else bankv[0:1, :]
            w = [sPv.at(pv_done[base + j * NG + NG - 1])]
            if jj >= 1:
                w.append(sBc.at(bc_done[jj - 1]))
            nm_recip[jj] = vop("vector", "reciprocal", sb_rrow[:], srow,
                               waits=tuple(w), inc=sNm)

        def emit_bcast(j):
            jj = p * NITER + j
            w = [sNm.at(nm_recip[jj])]
            if jj >= 1:
                w.append(sNm.at(nm_evac[jj - 1]))
            mm(psE[:, :], c_ones1[:], sb_rrow[:], waits=tuple(w), inc=sBc)
            bc_done[jj] = sBc.n

        def emit_norm(j):
            jj = p * NITER + j
            hl, qg = it_hq(j)
            h = h0 + hl
            bankv = psC if j % 2 == 0 else psD
            even = (hl % 2 == 0)
            nm_evac[jj] = vop("vector", "tensor_copy", sb_rb[:, 0:512],
                              psE[:, :], waits=(sBc.at(bc_done[jj]),), inc=sNm)
            r0 = 0 if even else 64
            fc = h // 2
            nm_tt[jj] = vop(
                "vector", "tensor_tensor",
                sb_att[r0:r0 + 64, fc * QT + qg * 512: fc * QT + (qg + 1) * 512],
                bankv[r0:r0 + 64, :], sb_rb[r0:r0 + 64, 0:512], ALU.mult, inc=sNm)

        emit_score(0)
        for jg in range(NITER * NG):
            j, g = divmod(jg, NG)
            if jg + 1 < NITER * NG:
                emit_score(jg + 1)
            emit_exp(jg)
            if g == 2 and j >= 1:
                emit_bcast(j - 1)
                emit_norm(j - 1)
            emit_pv(jg)
            if g == NG - 1:
                emit_recip(j)
        emit_bcast(NITER - 1)
        emit_norm(NITER - 1)
        gidg += NITER * NG
        last_nm = sNm.n
        last_pass_pe = sPe.n

    ATT_NM = sNm.n
    ATT_EX = sEx.n

    if LV <= 3:
        ob = d_out.bitcast(BF16).rearrange("(a p) n -> p a n", p=128)
        if LV == 1:
            dma("sync", ob[:, 0:8, :],
                sb_kp[:].rearrange("p (a n) -> p a n", n=2048), sSt,
                waits=(sVe.at(),) if qlim == 105 else (sVe.at(), sAc.at()))
            dma("sync", ob[:, 8:12, :],
                sb_qp[:].rearrange("p (a n) -> p a n", n=2048), sSt)
            dma("sync", ob[:, 12:14, :],
                sb_vp[:, 0:4096].rearrange("p (a n) -> p a n", n=2048), sSt)
        else:
            dma("sync", ob[:, 0:8, :],
                sb_att[:].rearrange("p (a n) -> p a n", n=2048), sSt,
                waits=(sNm.at(),))
        P.op("sync", None, waits=(sSt.at(),))
        with nc.Block() as block:
            P.emit(block)
        return nc

    for dc in range(8):
        dma("sync", sb_wot[:, dc * 1024:(dc + 1) * 1024],
            s_wbf[3, :, dc * 128:(dc + 1) * 128], sTp,
            waits=(sGw.at(), sNm.at(ATT_NM)) if dc == 0 else (), transpose=True)
    WOT_TP = sTp.n
    bo_ld = dma("sync", sb_rowE[:], d_bo, sLd, waits=(sPe.at(),))
    mm(psC[:, :], c_ones1[:], sb_rowE[0:1, 0:512],
       waits=(sLd.at(bo_ld), sNm.at(ATT_NM), sVe.at(), sAc.at()), inc=sPe)
    mm(psD[:, :], c_ones1[:], sb_rowE[0:1, 512:1024], inc=sPe)
    bo_pe = sPe.n
    vop("vector", "tensor_copy", sb_bob[:, 0:512], psC[:, :],
        waits=(sPe.at(bo_pe),))
    BOB_VE = vop("vector", "tensor_copy", sb_bob[:, 512:1024], psD[:, :], inc=sVe)

    ab_war = {id(psA): (), id(psB): ()}
    xq_war = {0: (), 1: ()}
    xf_st = {}
    prev_xfa = 0
    for tt in range(16):
        bank = psA if tt % 2 == 0 else psB
        xbuf = sb_xqt[:, (tt % 2) * E:(tt % 2 + 1) * E]
        xld = dma("sync", xbuf, d_xq[tt * 128:(tt + 1) * 128, :], sTp,
                  waits=xq_war[tt % 2])
        w = list(ab_war[id(bank)])
        if tt == 0:
            w += [sTp.at(WOT_TP), sEx.at(ATT_EX), sNm.at(ATT_NM)]
        for ic in range(8):
            for half in range(2):
                mm(bank[:, half * 512:(half + 1) * 512],
                   sb_att[:, ic * QT + tt * 128: ic * QT + (tt + 1) * 128],
                   sb_wot[:, ic * 1024 + half * 512: ic * 1024 + (half + 1) * 512],
                   start=(ic == 0), stop=(ic == 7),
                   waits=tuple(w) if (ic == 0 and half == 0) else (),
                   inc=sPe if (ic == 7 and half == 1) else None)
        opv = sPe.n
        vop("vector", "tensor_tensor", sb_t0a[:], bank[:], xbuf, ALU.add,
            waits=(sPe.at(opv), sTp.at(xld), sVe.at(BOB_VE)))
        t0v = vop("vector", "tensor_tensor", sb_t0a[:], sb_t0a[:], sb_bob[:],
                  ALU.add, inc=sVe)
        ab_war[id(bank)] = (sVe.at(t0v),)
        xq_war[tt % 2] = (sVe.at(t0v),)
        rv, _ = ln_stats_vec(sb_t0a[:], sb_scr[:],
                             waits_sum=(sVe.at(t0v), sAc.at(prev_xfa)),
                             waits_sq=(sVe.at(t0v),))
        xfa = act(sb_xf[:], sb_t0a[:], AF.Identity, bias=sb_svec[:, 9:10],
                  scale=sb_svec[:, 6:7], waits=(sVe.at(rv),), inc=sAc)
        prev_xfa = xfa
        xfbv = vop("vector", "tensor_copy", sb_xfb[:], sb_xf[:],
                   waits=(sAc.at(xfa),), inc=sVe)
        xlov = cvop("tensor_sub", sb_xlo[:], sb_xf[:], sb_xfb[:],
                    waits=(sVe.at(xfbv),))
        dma("scalar", s_xf32[tt * 128:(tt + 1) * 128, :], sb_xf[:], sSt,
            waits=(sAc.at(xfa),))
        dma("scalar", s_xfbf[tt * 128:(tt + 1) * 128, :], sb_xfb[:],
            sSt, waits=(sVe.at(xfbv),))
        xf_st[tt] = dma("scalar", s_xflo[tt * 128:(tt + 1) * 128, :], sb_xlo[:],
                        sSt, waits=(sVe.at(xlov),))
    if LV == 4:
        for tt in range(16):
            dma("sync", d_out[tt * 128:(tt + 1) * 128, :],
                s_xf32[tt * 128:(tt + 1) * 128, :], sSt, waits=(sSt.at(),))
        P.op("sync", None, waits=(sSt.at(),))
        with nc.Block() as block:
            P.emit(block)
        return nc
    PC_ST = sSt.n
    PC_VE = sVe.n
    PC_AC = sAc.n

    for dc in range(8):
        dma("sync", sb_xfm[:, dc * QT:(dc + 1) * QT],
            s_xfbf[:, dc * 128:(dc + 1) * 128], sTp,
            waits=(sSt.at(PC_ST),) if dc == 0 else (), transpose=True)
    for dc in range(8):
        dma("sync", sb_xfmlo[:, dc * QT:(dc + 1) * QT],
            s_xflo[:, dc * 128:(dc + 1) * 128], sTp, transpose=True)
    XFM_TP = sTp.n

    GWLO_VE = vop("vector", "tensor_sub", sb_gwlo[:], sb_gwf[:], sb_gw[:],
                  waits=(sLd.at(GWF_LD), sGw.at(G_GW)), inc=sVe)
    g = sb_gz
    gz_war = ()
    for tt in range(16):
        w = list(gz_war)
        if tt == 0:
            w += [sTp.at(XFM_TP), sGw.at(), sVe.at(PC_VE), sAc.at(PC_AC),
                  sVe.at(GWLO_VE)]
        for gi, (xsrc, wsrc) in enumerate([(sb_xfm, sb_gw), (sb_xfm, sb_gwlo),
                                           (sb_xfmlo, sb_gw)]):
            for dc in range(8):
                mm(psF[:, 0:NEXP],
                   xsrc[:, dc * QT + tt * 128: dc * QT + (tt + 1) * 128],
                   wsrc[:, dc * NEXP:(dc + 1) * NEXP],
                   start=(gi == 0 and dc == 0), stop=(gi == 2 and dc == 7),
                   waits=tuple(w) if (gi == 0 and dc == 0) else (),
                   inc=sPe if (gi == 2 and dc == 7) else None)
        zv = sPe.n
        cvop("reduce_max", g[:, 8:9], psF[:, 0:NEXP], axis=AX.X,
             waits=(sPe.at(zv),))
        m1v = cvop("tensor_scalar", g[:, 9:10], g[:, 8:9], -1.0, None, ALU.mult)
        v8 = sb_scr[:, 0:8]
        m8 = sb_scr[:, 8:16]
        cvop("tensor_scalar", v8, psF[:, 0:NEXP], g[:, 8:9], None, ALU.is_ge)
        cvop("scalar_tensor_tensor", v8, v8, -1e9, psF[:, 0:NEXP],
             ALU.mult, ALU.add)
        cvop("reduce_max", g[:, 10:11], v8, axis=AX.X)
        ge2 = cvop("tensor_scalar", m8, psF[:, 0:NEXP], g[:, 10:11], None,
                   ALU.is_ge)
        eza = act(g[:, 0:8], psF[:, 0:NEXP], AF.Exp, bias=g[:, 9:10],
                  waits=(sVe.at(ge2),), inc=sAc)
        cvop("tensor_tensor", v8, m8, g[:, 0:8], ALU.mult,
             waits=(sAc.at(eza),))
        cvop("reduce_sum", g[:, 11:12], v8, axis=AX.X)
        cvop("reciprocal", g[:, 12:13], g[:, 11:12])
        cv = cvop("tensor_scalar", g[:, 0:8], v8, g[:, 12:13], None, ALU.mult)
        trv = tr(psE[0:8, 0:128], g[:, 0:8], c_eyef[:], waits=(sVe.at(cv),),
                 inc=sPe)
        c8v = vop("vector", "tensor_copy", sb_c8[:, tt * 128:(tt + 1) * 128],
                  psE[0:8, 0:128], waits=(sPe.at(trv),), inc=sVe)
        gz_war = (sVe.at(c8v), sAc.at(eza))
    for i in range(16):
        vop("vector", "tensor_copy", sb_c8t[i][0:NEXP, :],
            sb_c8[:, i * 128:(i + 1) * 128], inc=sVe)
    GATES_VE = sVe.n
    GATES_AC = sAc.n
    GATES_PE = sPe.n
    if LV == 5:
        of = d_out.rearrange("(a b) n -> a (b n)", b=2)
        dma("sync", of[0:8, :], sb_c8[:], sSt, waits=(sVe.at(GATES_VE),))
        P.op("sync", None, waits=(sSt.at(),))
        with nc.Block() as block:
            P.emit(block)
        return nc

    def fn_pre_gelu(eh):
        return nc.scalar.preload_act_table(AF.Gelu)
    P.op("scalar", fn_pre_gelu)

    w1_ld = {}
    w2_ld = {}
    dma("sync", sb_w1[:].rearrange("p (dc hh) -> p dc hh", hh=HID),
        s_w1bf[0].rearrange("(dc p) hh -> p dc hh", p=128), sTp,
        waits=(sGf.at(), sTp.at(XFM_TP), sPe.at(GATES_PE)))
    w1_ld[0] = sTp.n
    dma("sync", sb_w2[:].rearrange("p (hc dd) -> p hc dd", dd=E),
        s_w2bf[0].rearrange("(hc p) dd -> p hc dd", p=128), sTp,
        waits=(sGf.at(), sPe.at(GATES_PE)))
    w2_ld[0] = sTp.n

    lng_ld = dma("sync", sb_rowE[:], d_lng, sLd, waits=(sPe.at(),))
    mm(psC[:, :], c_ones1[:], sb_rowE[0:1, 0:512],
       waits=(sLd.at(lng_ld), sVe.at(BOB_VE)), inc=sPe)
    mm(psD[:, :], c_ones1[:], sb_rowE[0:1, 512:1024], inc=sPe)
    gpe = sPe.n
    vop("vector", "tensor_copy", sb_lngb[:, 0:512], psC[:, :],
        waits=(sPe.at(gpe),))
    gve = vop("vector", "tensor_copy", sb_lngb[:, 512:1024], psD[:, :],
              inc=sVe)
    lnb_ld = dma("sync", sb_rowE[:], d_lnb, sLd, waits=(sPe.at(gpe),))
    mm(psC[:, :], c_ones1[:], sb_rowE[0:1, 0:512],
       waits=(sLd.at(lnb_ld), sVe.at(gve)), inc=sPe)
    mm(psD[:, :], c_ones1[:], sb_rowE[0:1, 512:1024], inc=sPe)
    gpe2 = sPe.n
    vop("vector", "tensor_copy", sb_lnbb[:, 0:512], psC[:, :],
        waits=(sPe.at(gpe2),))
    lnb_ve = vop("vector", "tensor_copy", sb_lnbb[:, 512:1024],
                 psD[:, :], inc=sVe)

    NSLAB = NQ_RUN * NEXP * NTS
    hsq_war3 = {0: (), 1: (), 2: ()}
    h_war = {hc: () for hc in range(16)}
    hp_war = {hc: () for hc in range(16)}
    t1_war = {0: (), 1: ()}
    t2_war = {0: (), 1: ()}
    gg_war = {0: (), 1: ()}
    fbc_war = (sAc.at(GATES_AC), sVe.at(GATES_VE))
    cb_war = (sVe.at(GATES_VE),)
    y_war = {id(psA): (sVe.at(PC_VE),), id(psB): (sVe.at(PC_VE),)}
    cd_war = {id(psC): (sVe.at(lnb_ve),), id(psD): (sVe.at(lnb_ve),)}
    fxq_war = {0: xq_war[0], 1: xq_war[1]}
    out_war = ()
    w1bar_done = {}
    prev_outa = 0
    t0f_war = ()
    srd_prev = GATES_VE
    fin_pending = []
    fin_gens = []
    mrow_ve = {}

    def fin_pump(k):
        while k > 0 and fin_gens:
            try:
                next(fin_gens[0])
                k -= 1
            except StopIteration:
                fin_gens.pop(0)

    rstd_v = {}
    stats_pe_s = {}
    w2_done = {}
    pa_cnt = {}
    mub_mm = 0
    rst_mm = 0
    mub_war = ()
    cbread_po = ()

    for i in range(NSLAB + 1):
        sP, sN = i, i - 1
        have1 = sP < NSLAB
        have2 = sN >= 0
        if have1:
            qp_, ep = sP // (NEXP * NTS), (sP // NTS) % NEXP
            tsp = sP % NTS
            itp = qp_ * NEXP + ep
            tok0p = qp_ * QTR + tsp * TSF
            if qp_ == 0 and tsp == 0:
                for dc in range(8):
                    cvop("reduce_sum",
                         sb_w1bar[:, ep * 8 + dc: ep * 8 + dc + 1],
                         sb_w1[:, dc * HID:(dc + 1) * HID], axis=AX.X,
                         waits=(sTp.at(w1_ld[itp]),) if dc == 0 else ())
                cvop("tensor_scalar", sb_w1bar[:, ep * 8:(ep + 1) * 8],
                     sb_w1bar[:, ep * 8:(ep + 1) * 8], 1.0 / HID, None,
                     ALU.mult)
                w1bar_done[ep] = cvop("tensor_copy",
                                      sb_w1barb[:, ep * 8:(ep + 1) * 8],
                                      sb_w1bar[:, ep * 8:(ep + 1) * 8])
        if have2:
            qn, en = sN // (NEXP * NTS), (sN // NTS) % NEXP
            tsn = sN % NTS
            itn = qn * NEXP + en
            tok0n = qn * QTR + tsn * TSF
            mm(psF[:, 0:TSF], c_sel8[:, en * 128:(en + 1) * 128],
               sb_c8[:, tok0n:tok0n + TSF],
               waits=(sVe.at(mrow_ve[sN]),) + tuple(cb_war), inc=sPe)
            cb_v = vop("vector", "tensor_copy", sb_cb[:], psF[:, 0:TSF],
                       waits=(sPe.at(),) + tuple(cbread_po), inc=sVe)
            mm(psF[:, 0:TSF], c_ones1[:], sb_mrow[:],
               waits=(sVe.at(cb_v),), inc=sPe)
            mub_mm = sPe.n
            mub_v = vop("vector", "tensor_copy", sb_mub[:], psF[:, 0:TSF],
                        waits=(sPe.at(mub_mm),) + tuple(mub_war), inc=sVe)
            mm(psF[:, 0:TSF], c_ones1[:], sb_yrow[:],
               waits=(sVe.at(rstd_v[sN]), sVe.at(mub_v)), inc=sPe)
            rst_mm = sPe.n
            rb_v = vop("vector", "tensor_copy", sb_rstb[:], psF[:, 0:TSF],
                       waits=(sPe.at(rst_mm),), inc=sVe)
            fbc_war = (sVe.at(rb_v),)
            cb_war = (sVe.at(cb_v),)
        sq_ac = {}
        pcm = {}
        w1mm_last = 0
        for hc in range(16):
            if have2:
                pA = vop("vector", "tensor_tensor",
                         sb_t1[:, (hc % 2) * TSF:(hc % 2 + 1) * TSF],
                         sb_h[:, hc * TSF:(hc + 1) * TSF], sb_mub[:],
                         ALU.subtract,
                         waits=((sVe.at(mub_v),) if hc == 0 else ()) +
                         tuple(t1_war[hc % 2]), inc=sVe)
                pa_cnt[hc] = pA
                pB = vop("vector", "scalar_tensor_tensor",
                         sb_t2[:, (hc % 2) * TSF:(hc % 2 + 1) * TSF],
                         sb_t1[:, (hc % 2) * TSF:(hc % 2 + 1) * TSF],
                         sb_g1v[:, en * 16 + hc: en * 16 + hc + 1],
                         sb_rstb[:], ALU.mult, ALU.mult,
                         waits=((sVe.at(rb_v),) if hc == 0 else ()) +
                         tuple(t2_war[hc % 2]), inc=sVe)
                t1_war[hc % 2] = (sVe.at(pB),)
                pG = act(sb_gg[:, (hc % 2) * TSF:(hc % 2 + 1) * TSF],
                         sb_t2[:, (hc % 2) * TSF:(hc % 2 + 1) * TSF],
                         AF.Gelu,
                         bias=sb_lb1v[:, en * 16 + hc: en * 16 + hc + 1],
                         waits=(sVe.at(pB),) + tuple(gg_war[hc % 2]), inc=sAc)
                t2_war[hc % 2] = (sAc.at(pG),)
                pC = vop("gpsimd", "tensor_tensor",
                         sb_hp[:, hc * TSF:(hc + 1) * TSF],
                         sb_gg[:, (hc % 2) * TSF:(hc % 2 + 1) * TSF],
                         sb_cb[:], ALU.mult,
                         waits=(sAc.at(pG),) +
                         ((sVe.at(cb_v),) if hc == 0 else ()) +
                         tuple(hp_war[hc]), inc=sPo)
                gg_war[hc % 2] = (sPo.at(pC),)
                pcm[hc] = pC
                if hc == 15:
                    mub_war = (sVe.at(pA),)
                    cbread_po = (sPo.at(pC),)
            if have1:
                bank = psC if hc % 2 == 0 else psD
                w = list(cd_war[id(bank)])
                if hc == 0:
                    w.append(sTp.at(w1_ld[itp]))
                    if sP == 0:
                        w.append(sTp.at(XFM_TP))
                for dc in range(8):
                    mm(bank[:, 0:TSF],
                       sb_w1[:, dc * HID + hc * 128: dc * HID + (hc + 1) * 128],
                       sb_xfm[:, dc * QT + tok0p: dc * QT + tok0p + TSF],
                       start=(dc == 0), stop=(dc == 7),
                       waits=tuple(w) if dc == 0 else (),
                       inc=sPe if dc == 7 else None)
                w1mm_last = sPe.n
                hev = vop("vector", "tensor_scalar",
                          sb_h[:, hc * TSF:(hc + 1) * TSF], bank[:, 0:TSF],
                          sb_b1v[:, ep * 16 + hc: ep * 16 + hc + 1], None,
                          ALU.add,
                          waits=(sPe.at(w1mm_last),) +
                          (() if have2 else tuple(h_war[hc])), inc=sVe)
                cd_war[id(bank)] = (sVe.at(hev),)
                sqd = act(sb_hsq[:, (hc % 3) * TSF:(hc % 3 + 1) * TSF],
                          sb_h[:, hc * TSF:(hc + 1) * TSF], AF.Square,
                          waits=(sVe.at(hev),) + tuple(hsq_war3[hc % 3]),
                          inc=sAc)
                sq_ac[hc] = sqd
                if hc == 0:
                    for dc in range(8):
                        mm(psF[0:1, 0:TSF],
                           sb_w1barb[:, ep * 8 + dc: ep * 8 + dc + 1],
                           sb_xfm[:, dc * QT + tok0p: dc * QT + tok0p + TSF],
                           start=(dc == 0), stop=(dc == 7), tile=(0, 0),
                           waits=((sVe.at(w1bar_done[ep]),) + tuple(fbc_war))
                           if dc == 0 else (),
                           inc=sPe if dc == 7 else None)
                hs = hc - 2
                if hs >= 0:
                    ssw = (sAc.at(sq_ac[hs]),)
                    if hs == 0:
                        ssw += (sVe.at(srd_prev),)
                    mm(psE[32:33, 0:TSF], c_ones128[:],
                       sb_hsq[:, (hs % 3) * TSF:(hs % 3 + 1) * TSF],
                       start=(hs == 0), stop=(hs == 15), tile=(0, 32),
                       waits=ssw, inc=sPe)
                    hsq_war3[hs % 3] = (sPe.at(sPe.n),)
            if hc % 4 == 3:
                fin_pump(2)
        if have1:
            for hs in (14, 15):
                mm(psE[32:33, 0:TSF], c_ones128[:],
                   sb_hsq[:, (hs % 3) * TSF:(hs % 3 + 1) * TSF],
                   start=False, stop=(hs == 15),
                   tile=(0, 32), waits=(sAc.at(sq_ac[hs]),), inc=sPe)
                hsq_war3[hs % 3] = (sPe.at(sPe.n),)
            stats_pe_s[sP] = sPe.n
            cvop("tensor_scalar", sb_mrow[:], psF[0:1, 0:TSF],
                 sb_b1bar[0:1, ep:ep + 1], None, ALU.add,
                 waits=(sPe.at(stats_pe_s[sP]), sPe.at(mub_mm)))
            mrow_ve[sP] = sVe.n
            srd_prev = cvop("tensor_scalar", sb_srow[:],
                            psE[32:33, 0:TSF], 1.0 / HID, EPS, ALU.mult,
                            ALU.add)
            cvop("tensor_mul", sb_vrow[:], sb_mrow[:], sb_mrow[:])
            cvop("tensor_sub", sb_vrow[:], sb_srow[:], sb_vrow[:])
            rsqrt_rows(sb_vrow[:], sb_yrow[:], sb_y2row[:],
                       waits=(sPe.at(rst_mm),))
            rstd_v[sP] = sVe.n
        if have2:
            for tt in range(2):
                gt4 = tsn * 2 + tt
                ybank = psA if (qn * 4 + gt4) % 2 == 0 else psB
                w = list(y_war[id(ybank)])
                w.append(sTp.at(w2_ld[itn]))
                w.append(sPo.at(pcm[15]))
                for hc in range(16):
                    whc = tuple(w)
                    w = []
                    for half in range(2):
                        mm(ybank[:, half * 512:(half + 1) * 512],
                           sb_hp[:, hc * TSF + tt * 128:
                                 hc * TSF + tt * 128 + 128],
                           sb_w2[:, hc * E + half * 512:
                                 hc * E + (half + 1) * 512],
                           start=(hc == 0), stop=(hc == 15),
                           waits=whc if half == 0 else (),
                           inc=sPe if (hc == 15 and half == 1) else None)
                w2mm = sPe.n
                ydst = sb_y[:, gt4 * E:(gt4 + 1) * E]
                if en == 0:
                    yv = vop("vector", "tensor_copy", ydst, ybank[:],
                             waits=(sPe.at(w2mm),), inc=sVe)
                else:
                    yv = vop("vector", "tensor_tensor", ydst, ybank[:],
                             ydst, ALU.add, waits=(sPe.at(w2mm),), inc=sVe)
                y_war[id(ybank)] = (sVe.at(yv),)
            w2_done[sN] = sPe.n
            for hc in range(16):
                hp_war[hc] = (sPe.at(w2_done[sN]),)
            if tsn == 1 and itn + 1 < NQ_RUN * NEXP and (itn + 1) not in w2_ld:
                ne2 = (itn + 1) % NEXP
                dma("sync", sb_w2[:].rearrange("p (hc dd) -> p hc dd", dd=E),
                    s_w2bf[ne2].rearrange("(hc p) dd -> p hc dd", p=128), sTp,
                    waits=(sGf.at(), sPe.at(w2_done[sN])))
                w2_ld[itn + 1] = sTp.n
        if have1 and tsp == 1 and itp + 1 < NQ_RUN * NEXP:
            ne1 = (itp + 1) % NEXP
            wld = [sGf.at(), sPe.at(w1mm_last)]
            if qp_ == 0:
                wld.append(sVe.at(w1bar_done[ep]))
            dma("sync", sb_w1[:].rearrange("p (dc hh) -> p dc hh", hh=HID),
                s_w1bf[ne1].rearrange("(dc p) hh -> p dc hh", p=128), sTp,
                waits=tuple(wld))
            w1_ld[itp + 1] = sTp.n

        if have2 and en == NEXP - 1 and tsn == NTS - 1:
            fin_pending += [(qn, 0), (qn, 1), (qn, 2), (qn, 3)]

        def emit_fin_tile(q, tt):
            nonlocal prev_outa, t0f_war, out_war
            if True:
                gt = q * 4 + tt
                ybank = psA if gt % 2 == 0 else psB
                xbuf = sb_xqt[:, (gt % 2) * E:(gt % 2 + 1) * E]
                xld = dma("sync", xbuf, s_xf32[gt * 128:(gt + 1) * 128, :],
                          sTp, waits=(sSt.at(xf_st[gt]),) +
                          tuple(fxq_war[gt % 2]))
                w = list(y_war[id(ybank)]) + [sVe.at(GATES_VE), sGw.at()]
                mm(ybank[:, 0:512], sb_c8t[gt][:],
                   sb_b2[:, 0:512], waits=tuple(w), inc=sPe)
                mm(ybank[:, 512:1024], sb_c8t[gt][:],
                   sb_b2[:, 512:1024], inc=sPe)
                b2pe = sPe.n
                ydst = sb_y[:, tt * E:(tt + 1) * E]
                vop("vector", "tensor_tensor", sb_t0f[:], ybank[:], ydst,
                    ALU.add,
                    waits=(sPe.at(b2pe), sVe.at(lnb_ve)) + tuple(t0f_war))
                t0v = vop("vector", "tensor_tensor", sb_t0f[:], sb_t0f[:],
                          xbuf, ALU.add, waits=(sTp.at(xld),), inc=sVe)
                y_war[id(ybank)] = (sVe.at(t0v),)
                yield
                rv, _ = ln_stats_vec(sb_t0f[:], sb_scrf[:],
                                     waits_sum=(sVe.at(t0v),
                                                sAc.at(prev_outa)),
                                     waits_sq=(sVe.at(t0v),))
                t2a = act(sb_scrf[:], sb_t0f[:], AF.Identity,
                          bias=sb_svec[:, 9:10], scale=sb_svec[:, 6:7],
                          waits=(sVe.at(rv),), inc=sAc)
                prev_outa = t2a
                yield
                vop("gpsimd", "tensor_tensor", sb_t0f[:], sb_scrf[:],
                    sb_lngb[:], ALU.mult, waits=(sAc.at(t2a),))
                vop("gpsimd", "tensor_tensor", sb_t0f[:], sb_t0f[:],
                    sb_lnbb[:], ALU.add)
                t5v = vop("gpsimd", "tensor_tensor", sb_t0f[:], sb_t0f[:],
                          xbuf, ALU.add, inc=sPo)
                fxq_war[gt % 2] = (sPo.at(t5v),)
                sqg = vop("gpsimd", "tensor_tensor", sb_scrf[:], sb_t0f[:],
                          sb_t0f[:], ALU.mult, inc=sPo)
                cvop("reduce_sum", sb_svec[:, 1:2], sb_scrf[:], axis=AX.X,
                     waits=(sPo.at(sqg),))
                cvop("reduce_sum", sb_svec[:, 0:1], sb_t0f[:], axis=AX.X,
                     waits=(sPo.at(t5v),))
                cvop("tensor_scalar", sb_svec[:, 2:3], sb_svec[:, 0:1],
                     1.0 / E, None, ALU.mult)
                cvop("tensor_scalar", sb_svec[:, 3:4], sb_svec[:, 1:2],
                     1.0 / E, EPS, ALU.mult, ALU.add)
                cvop("tensor_mul", sb_svec[:, 4:5], sb_svec[:, 2:3],
                     sb_svec[:, 2:3])
                cvop("tensor_sub", sb_svec[:, 5:6], sb_svec[:, 3:4],
                     sb_svec[:, 4:5])
                rsqrt_rows(sb_svec[:, 5:6], sb_svec[:, 6:7], sb_svec[:, 7:8])
                cvop("tensor_mul", sb_svec[:, 9:10], sb_svec[:, 2:3],
                     sb_svec[:, 6:7])
                rv2 = cvop("tensor_scalar", sb_svec[:, 9:10],
                           sb_svec[:, 9:10], -1.0, None, ALU.mult)
                yield
                outa = act(sb_outt[:], sb_t0f[:], AF.Identity,
                           bias=sb_svec[:, 9:10], scale=sb_svec[:, 6:7],
                           waits=(sVe.at(rv2),) + tuple(out_war), inc=sAc)
                prev_outa = outa
                t0f_war = (sAc.at(outa),)
                dma("scalar", d_out[gt * 128:(gt + 1) * 128, :], sb_outt[:],
                    sSt, waits=(sAc.at(outa),))
                out_war = (sSt.at(),)

        while fin_pending:
            fin_gens.append(emit_fin_tile(*fin_pending.pop(0)))

    fin_pump(1 << 30)

    P.op("sync", None, waits=(sSt.at(),))

    with nc.Block() as block:
        P.emit(block)
    return nc


GATESVE = None


def kernel(**inputs):
    x = np.asarray(inputs["L0_attn_x"])
    com = {
        "Wq": np.ascontiguousarray(inputs["L0_attn_Wq"], np.float32),
        "bq": np.ascontiguousarray(inputs["L0_attn_bq"], np.float32).reshape(1, E),
        "Wk": np.ascontiguousarray(inputs["L0_attn_Wk"], np.float32),
        "bk": np.ascontiguousarray(inputs["L0_attn_bk"], np.float32).reshape(1, E),
        "Wv": np.ascontiguousarray(inputs["L0_attn_Wv"], np.float32),
        "bv": np.ascontiguousarray(inputs["L0_attn_bv"], np.float32).reshape(1, E),
        "Wo": np.ascontiguousarray(inputs["L0_attn_Wo"], np.float32),
        "bo": np.ascontiguousarray(inputs["L0_attn_bo"], np.float32).reshape(1, E),
        "J": np.ascontiguousarray(inputs["L0_attn_J_attn"], np.float32),
        "lam": np.asarray(inputs["L0_attn_lambda_attn"], np.float32).reshape(1, 1),
        "gw": np.ascontiguousarray(inputs["L0_ffn_gate_w"], np.float32),
        "w1": np.ascontiguousarray(inputs["L0_ffn_w1"], np.float32),
        "b1": np.ascontiguousarray(inputs["L0_ffn_b1"], np.float32),
        "g1": np.ascontiguousarray(inputs["L0_ffn_ln1_g"], np.float32),
        "lb1": np.ascontiguousarray(inputs["L0_ffn_ln1_b"], np.float32),
        "w2": np.ascontiguousarray(inputs["L0_ffn_w2"], np.float32),
        "b2": np.ascontiguousarray(inputs["L0_ffn_b2"], np.float32),
        "lng": np.ascontiguousarray(inputs["L0_ffn_ln_g"], np.float32).reshape(1, E),
        "lnb": np.ascontiguousarray(inputs["L0_ffn_ln_b"], np.float32).reshape(1, E),
    }
    in_maps = []
    for c in range(8):
        b = c // 2
        q0 = (c % 2) * QT
        m = dict(com)
        m["xq"] = np.ascontiguousarray(x[b, q0:q0 + QT], np.float32)
        m["xkv"] = np.ascontiguousarray(x[b], np.float32)
        in_maps.append(m)
    nc = build_program()
    res = run_bass_kernel_spmd(nc, in_maps, core_ids=list(range(8)))
    out = np.empty((B, S, E), np.float32)
    for c in range(8):
        b = c // 2
        q0 = (c % 2) * QT
        out[b, q0:q0 + QT] = np.asarray(res.results[c]["out"])
    return out
